# version 14
# baseline (speedup 1.0000x reference)
import math
from contextlib import ExitStack

import numpy as np

import concourse.bass as bass
import concourse.mybir as mybir
import concourse.bass_utils as bass_utils

F32 = mybir.dt.float32
BF16 = mybir.dt.bfloat16
AF = mybir.ActivationFunctionType
ALU = mybir.AluOpType

D = 1024
I = 2048
NH = 16
HD = 64
S = 2048
BPC = 2
T = BPC * S
NE = 8
NCORE = 8
QT = 1024
W = 512

f32 = np.float32



def _prep_gu(Wg, Wu):
    g = Wg.reshape(16, 128, 8, 128).transpose(0, 3, 2, 1)
    u = Wu.reshape(16, 128, 8, 128).transpose(0, 3, 2, 1)
    gu = np.stack([g, u], axis=2)
    return np.ascontiguousarray(gu.reshape(16, 128, 2048))


def _prep_dn(Wd):
    d = Wd.reshape(8, 128, 16, 128)
    d = d.transpose(2, 0, 3, 1)
    d = d.reshape(2, 8, 8, 128, 128)
    d = d.transpose(0, 2, 3, 1, 4)
    return np.ascontiguousarray(d.reshape(2, 8, 128, 1024))


def _prep_qkv(Wq, Wk, Wv):
    def t(Wm):
        return Wm.reshape(8, 128, 8, 128).transpose(0, 3, 2, 1)
    a = np.stack([t(Wq), t(Wk), t(Wv)], axis=2)
    return np.ascontiguousarray(a.reshape(8, 128, 3072))


def _prep_wo(Wo):
    a = Wo.reshape(8, 128, 8, 128)
    a = a.transpose(2, 3, 0, 1)
    a = a.reshape(8, 2, 64, 8, 128)
    a = a.transpose(0, 2, 1, 3, 4)
    return np.ascontiguousarray(a.reshape(8, 64, 2048))


def _prep_bias_cols(b):
    return np.ascontiguousarray(b.reshape(8, 128).T)


def _prep_rope():
    a = np.arange(0, D, 2, dtype=f32) * f32(-math.log(10000.0) / D)
    freq = np.exp(a).astype(f32)
    ang = (np.arange(S, dtype=f32)[:, None] * freq[None, :]).astype(f32)
    cos = np.cos(ang).astype(f32).T
    sin = np.sin(ang).astype(f32).T
    return (np.ascontiguousarray(cos.reshape(4, 128, S)),
            np.ascontiguousarray(sin.reshape(4, 128, S)))



ENGS = ("sync", "gpsimd", "tensor", "vector", "scalar")


class Chan:
    __slots__ = ("sem", "n")

    def __init__(self):
        self.sem = None
        self.n = 0


class Prog:

    def __init__(self):
        self.ops = {e: [] for e in ENGS}
        self.auto = {}

    def put(self, eng, fop, ch=None, k=1, selfdep=True):
        if eng in ("vector", "scalar"):
            auto = self.auto[eng]
            if selfdep:
                self.wait(eng, auto, auto.n)
            ch = auto
            k = 1
        if ch is None:
            self.ops[eng].append(fop)
            return None
        ch.n += k
        sem, kk = ch.sem, k

        def g(e, fop=fop, sem=sem, kk=kk):
            fop(e).then_inc(sem, kk)
        self.ops[eng].append(g)
        return ch.n

    def wait(self, eng, ch, val):
        if val is None or val <= 0:
            return
        sem = ch.sem

        def g(e, sem=sem, val=val):
            e.wait_ge(sem, val)
        self.ops[eng].append(g)


def op_dma(dst, src, **kw):
    def fop(e):
        return e.dma_start(dst, src, **kw)
    return fop


def op_mm(out, lhsT, rhs, start, stop):
    def fop(e):
        return e.matmul(out, lhsT, rhs, start=start, stop=stop)
    return fop


def op_tp(out, in_, ident):
    def fop(e):
        return e.transpose(out, in_, ident)
    return fop


def op_act(out, in_, func, bias=None, scale=None):
    kw = {}
    if bias is not None:
        kw["bias"] = bias
    if scale is not None:
        kw["scale"] = scale

    def fop(e):
        return e.activation(out, in_, func, **kw)
    return fop


def op_tt(out, in0, in1, op):
    def fop(e):
        return e.tensor_tensor(out, in0, in1, op)
    return fop


def op_ts(out, in0, s1, op0):
    def fop(e):
        return e.tensor_scalar(out, in0, s1, None, op0)
    return fop


def op_stt(out, in0, s, in1, op0, op1):
    def fop(e):
        return e.scalar_tensor_tensor(out, in0, s, in1, op0, op1)
    return fop


def op_copy(out, in_):
    def fop(e):
        return e.tensor_copy(out, in_)
    return fop


def op_scopy(out, in_):
    def fop(e):
        return e.copy(out, in_)
    return fop


def op_max8(out, in_):
    def fop(e):
        return e.max(out, in_)
    return fop


def op_recip(out, in_, nc=None):
    def fop(e):
        if nc is not None:
            with nc.allow_low_precision(reason="softmax 1/denom -> bf16 bcast row"):
                return e.reciprocal(out, in_)
        return e.reciprocal(out, in_)
    return fop


def op_memset(ap_, v):
    def fop(e):
        return e.memset(ap_, v)
    return fop



def build_program(cfg=None):
    cfg = dict(cfg or {})
    do_ffn0 = cfg.get("ffn0", True)
    do_attn = cfg.get("attn", True)
    do_ffn1 = cfg.get("ffn1", True)
    n_exp = cfg.get("n_experts", 1 + NE)

    nc = bass.Bass("TRN2", target_bir_lowering=False)

    def dram(name, shape, dt, kind=None):
        if kind is None:
            return nc.dram_tensor(name, shape, dt).ap()
        return nc.dram_tensor(name, shape, dt, kind=kind).ap()

    x_d = dram("x", [T, D], F32, "ExternalInput")
    y_d = dram("y", [T, D], F32, "ExternalOutput")
    gu_d = dram("gu", [9, 16, 128, 2048], F32, "ExternalInput")
    dn_d = dram("dn", [9, 2, 8, 128, 1024], F32, "ExternalInput")
    qkv_d = dram("qkv", [8, 128, 3072], F32, "ExternalInput")
    wo_d = dram("wo", [8, 64, 2048], F32, "ExternalInput")
    wr_d = dram("wr", [128, 64], F32, "ExternalInput")
    rbb_d = dram("rbb", [128, 8], F32, "ExternalInput")
    bq_d = dram("bq", [128, 8], F32, "ExternalInput")
    bk_d = dram("bk", [128, 8], F32, "ExternalInput")
    bv_d = dram("bvb", [128, D], F32, "ExternalInput")
    bo_d = dram("bo", [128, 8], F32, "ExternalInput")
    cos_d = dram("cosd", [4, 128, S], F32, "ExternalInput")
    sin_d = dram("sind", [4, 128, S], F32, "ExternalInput")
    id_d = dram("ident", [128, 128], F32, "ExternalInput")
    ones_d = dram("onesd", [128, 128], F32, "ExternalInput")
    bnc_st_d = dram("bnc_st", [8, 1024], BF16)
    bnc_cw_d = dram("bnc_cw", [8, 1024], BF16)
    gu_bf_d = dram("gu_bf", [9, 16, 128, 2048], BF16)
    dn_bf_d = dram("dn_bf", [9, 2, 8, 128, 1024], BF16)
    dbg_on = cfg.get("dbg", False)
    if dbg_on:
        dbgcw_d = dram("dbgcw", [8, 1024], BF16, "ExternalOutput")
        dbgcol_d = dram("dbgcol", [128, 24], F32, "ExternalOutput")

    P = Prog()
    es = ExitStack()

    def sb(name, shape, dt, st=None):
        return (st or es).enter_context(nc.sbuf_tensor(name, shape, dt))

    r = sb("r", [128, 8 * T], F32)
    ident32 = sb("ident32", [128, 128], F32)
    ident_bf = sb("ident_bf", [128, 128], BF16)
    ones_bf = sb("ones_bf", [128, 128], BF16)
    ones32 = sb("ones32", [128, 128], F32)
    onescol_bf = sb("onescol_bf", [128, 1], BF16)
    wr_sb = sb("wr_sb", [128, 64], F32)
    rb_b = sb("rb_b", [128, 8], F32)
    sb8 = sb("sb8", [128, 8], F32)
    s8row = sb("s8row", [1, 8], F32)
    bq_c = sb("bq_c", [128, 8], F32)
    bk_c = sb("bk_c", [128, 8], F32)
    bo_c = sb("bo_c", [128, 8], F32)
    bv_b = sb("bv_b", [128, D], BF16)
    stats_row = sb("stats_row", [1, 1024], BF16)
    srow = sb("srow", [1, 512], F32)
    statsb = sb("statsb", [128, 1024], BF16)
    xbt = sb("xbt", [128, 1024], BF16)
    sqt = sb("sqt", [128, 1024], BF16)

    pbp = [es.enter_context(nc.psum_tensor(f"pbp{i}", [128, 1024], F32)) for i in range(4)]

    def pbv(i):
        return pbp[i // 2][:, (i % 2) * 512:(i % 2) * 512 + 512]
    pb = [pbv(i) for i in range(8)]
    tp_big = pbp[3][0:8, 0:512].bitcast(BF16)

    c_pe, c_ve, c_sc = Chan(), Chan(), Chan()
    dch = {k: Chan() for k in
           ("gu0", "gu1", "gu2", "dn0", "dn1", "dn2",
            "gus0", "gus1", "gus2", "dns0", "dns1", "dns2",
            "wb0", "wb1", "wb2", "wb3", "wb4", "wb5",
            "qkv", "wo", "cwb0", "cwb1",
            "bout", "sbst", "col", "cs", "xin0", "xin1", "yout0", "yout1",
            "const", "constg", "cwout")}
    for i, ch in enumerate([c_pe, c_ve, c_sc] + list(dch.values())):
        ch.sem = es.enter_context(nc.semaphore(f"sem{i}"))
    P.auto["vector"] = c_ve
    P.auto["scalar"] = c_sc

    def rv(c, t0, n):
        return r[:, c * T + t0: c * T + t0 + n]

    def dma_quiesce(eng):
        for ch in dch.values():
            P.wait(eng, ch, ch.n)

    def engine_guard(eng):
        P.wait(eng, c_pe, c_pe.n)
        P.wait(eng, c_ve, c_ve.n)
        P.wait(eng, c_sc, c_sc.n)

    P.put("sync", op_dma(ident32[:], id_d), dch["const"], 16)
    P.put("sync", op_dma(ones32[:], ones_d), dch["const"], 16)
    P.put("sync", op_dma(wr_sb[:], wr_d), dch["const"], 16)
    P.put("sync", op_dma(rb_b[:], rbb_d), dch["const"], 16)
    P.put("sync", op_dma(bq_c[:], bq_d), dch["const"], 16)
    P.put("sync", op_dma(bk_c[:], bk_d), dch["const"], 16)
    P.put("sync", op_dma(bo_c[:], bo_d), dch["const"], 16)
    P.put("gpsimd", op_dma(ones_bf[:], ones_d), dch["constg"], 16)
    P.put("gpsimd", op_dma(bv_b[:], bv_d), dch["constg"], 16)
    P.put("gpsimd", op_dma(ident_bf[:], id_d), dch["constg"], 16)
    P.put("gpsimd", op_dma(onescol_bf[:], ones_d[:, 0:1],
          allow_slow_non_contiguous=True), dch["constg"], 16)
    n_const = dch["const"].n
    n_constg = dch["constg"].n

    P.wait("tensor", dch["const"], n_const)
    P.wait("tensor", dch["constg"], n_constg)
    for kc in range(8):
        v_pe = P.put("tensor", op_mm(pb[1][0:1, 0:8], ones32[:, 0:1],
                                     wr_sb[:, kc * 8:kc * 8 + 8], kc == 0, kc == 7),
                     c_pe if kc == 7 else None)
    P.wait("vector", c_pe, v_pe)
    vvm = P.put("vector", op_copy(s8row[:], pb[1][0:1, 0:8]), c_ve)
    P.wait("tensor", c_ve, vvm)
    v_pe = P.put("tensor", op_mm(pb[2][:, 0:8], ones32[0:1, :], s8row[0:1, :],
                                 True, True), c_pe)
    P.wait("vector", c_pe, v_pe)
    P.put("vector", op_copy(sb8[:], pb[2][:, 0:8]), c_ve)

    io_st = ExitStack()
    xio = sb("xio", [128, 2048], F32, io_st)
    P.wait("tensor", c_ve, c_ve.n)
    pe_io = {}
    ve_io = {}
    for tt in range(32):
        sl = tt % 2
        if tt >= 2:
            P.wait("sync", c_pe, pe_io[tt - 2])
        xch = dch["xin%d" % sl]
        vx = P.put("sync", op_dma(xio[:, sl * 1024:(sl + 1) * 1024],
                                  x_d[tt * 128:(tt + 1) * 128, :]), xch, 16)
        P.wait("tensor", xch, vx)
        for c in range(8):
            bank = pb[c % 4]
            prev = (tt, c - 4) if c >= 4 else (tt - 1, c + 4)
            if prev in ve_io:
                P.wait("tensor", c_ve, ve_io[prev])
            m = P.put("tensor", op_tp(bank[:, 0:128],
                                      xio[:, sl * 1024 + c * 128: sl * 1024 + (c + 1) * 128],
                                      ident32[:]), c_pe)
            if c == 7:
                pe_io[tt] = m
            P.wait("vector", c_pe, m)
            ve_io[(tt, c)] = P.put("vector", op_copy(rv(c, tt * 128, 128), bank[:, 0:128]), c_ve)

    st_state = {"bout_prev": 0, "srow_sc_prev": 0, "pe6": None, "pe7": None,
                "statsb_war": None, "pb56_war": None}

    def stats_compute(w, cols=None, guard_ve=None):
        t0 = w * W
        P.wait("vector", dch["bout"], st_state["bout_prev"])
        P.wait("vector", c_sc, st_state["srow_sc_prev"])
        P.wait("tensor", c_ve, st_state["pb56_war"])
        P.wait("tensor", c_ve, st_state.get("tail_ve"))
        if guard_ve is not None:
            P.wait("tensor", c_ve, guard_ve)
        P.wait("sync", dch["sbst"], dch["sbst"].n)
        pe_m = {}
        for c in range(8):
            sl = c % 2
            if c >= 2:
                P.wait("vector", c_pe, pe_m[c - 2])
            elif st_state["pe6"] is not None:
                P.wait("vector", c_pe, st_state["pe6"] if c == 0 else st_state["pe7"])
            xsl = xbt[:, sl * 512:(sl + 1) * 512]
            qsl = sqt[:, sl * 512:(sl + 1) * 512]
            P.put("vector", op_copy(xsl, rv(c, t0, W)))
            vvm = P.put("vector", op_tt(qsl, xsl, xsl, ALU.mult), c_ve)
            P.wait("tensor", c_ve, vvm)
            P.put("tensor", op_mm(pb[6][0:1, :], onescol_bf[:], xsl, c == 0, c == 7))
            pe_m[c] = P.put("tensor", op_mm(pb[7][0:1, :], onescol_bf[:], qsl,
                                            c == 0, c == 7), c_pe)
        st_state["pe6"], st_state["pe7"] = pe_m[6], pe_m[7]
        return stats_tail(w, cols)

    def stats_stage(w, arena):
        t0 = w * W
        P.wait("vector", c_pe, state["last_gu_mark"])
        mk = None
        for c in range(8):
            xsl = arena[:, c * 1024: c * 1024 + 512]
            qsl = arena[:, c * 1024 + 512: (c + 1) * 1024]
            P.put("vector", op_copy(xsl, rv(c, t0, W)))
            mk = P.put("vector", op_tt(qsl, xsl, xsl, ALU.mult), c_ve)
        return mk

    def stats_mms(w, arena, stage_mark, cols=None, guard_ve=None):
        P.wait("vector", dch["bout"], st_state["bout_prev"])
        P.wait("vector", c_sc, st_state["srow_sc_prev"])
        P.wait("tensor", c_ve, st_state["pb56_war"])
        P.wait("tensor", c_ve, st_state.get("tail_ve"))
        if guard_ve is not None:
            P.wait("tensor", c_ve, guard_ve)
        P.wait("tensor", c_ve, stage_mark)
        P.wait("sync", dch["sbst"], dch["sbst"].n)
        for c in range(8):
            P.put("tensor", op_mm(pb[6][0:1, :], onescol_bf[:],
                                  arena[:, c * 1024: c * 1024 + 512], c == 0, c == 7))
            pe_m = P.put("tensor", op_mm(pb[7][0:1, :], onescol_bf[:],
                                         arena[:, c * 1024 + 512: (c + 1) * 1024],
                                         c == 0, c == 7), c_pe if c == 7 else None)
        st_state["pe6"] = st_state["pe7"] = pe_m
        return stats_tail(w, cols)

    def stats_tail(w, cols):
        P.wait("vector", c_pe, st_state["pe7"])
        P.put("vector", op_ts(stats_row[0:1, 0:512], pb[6][0:1, :], 1.0 / D, ALU.mult))
        P.put("vector", op_stt(srow[0:1, :], stats_row[0:1, 0:512], 1.0,
                               stats_row[0:1, 0:512], ALU.mult, ALU.mult))
        vvm = P.put("vector", op_stt(srow[0:1, :], pb[7][0:1, :], 1.0 / D, srow[0:1, :],
                                     ALU.mult, ALU.subtract), c_ve)
        st_state["tail_ve"] = vvm
        P.wait("scalar", c_ve, vvm)
        P.put("scalar", op_act(srow[0:1, :], srow[0:1, :], AF.Ln, bias=f32(1e-5)))
        sc = P.put("scalar", op_act(stats_row[0:1, 512:1024], srow[0:1, :], AF.Exp,
                                    scale=f32(-0.5)), c_sc)
        st_state["srow_sc_prev"] = sc
        P.wait("sync", c_sc, sc)
        st_state["bout_prev"] = P.put("sync", op_dma(bnc_st_d[w:w + 1, :], stats_row[:]),
                                      dch["bout"], 16)
        P.wait("sync", dch["bout"], st_state["bout_prev"])
        if cols is not None:
            mcol, rcol, wq = cols
            if state["col_read"] is not None:
                P.wait("gpsimd", c_ve, state["col_read"])
            P.wait("gpsimd", dch["bout"], st_state["bout_prev"])
            src_m = bnc_st_d[w:w + 1, 0:512].rearrange("o (a p) -> (o p) a", p=128)
            src_r = bnc_st_d[w:w + 1, 512:1024].rearrange("o (a p) -> (o p) a", p=128)
            P.put("gpsimd", op_dma(mcol[:, wq * 4:wq * 4 + 4], src_m,
                                   allow_slow_non_contiguous=True), dch["col"], 16)
            P.put("gpsimd", op_dma(rcol[:, wq * 4:wq * 4 + 4], src_r,
                                   allow_slow_non_contiguous=True), dch["col"], 16)
        return st_state["bout_prev"]

    def statsb_load(w):
        P.wait("sync", dch["bout"], stats_done[w])
        P.wait("sync", c_ve, st_state["statsb_war"])
        return P.put("sync", op_dma(statsb[:],
                                    bnc_st_d[w:w + 1, :].to_broadcast((128, 1024))),
                     dch["sbst"], 16)

    stats_done = {}

    gu_use = {}
    dn_use = {}
    cwb_use = {}
    state = {"col_read": None, "cwout_prev": 0, "last_gu_mark": None, "gu_rr": 0, "dn_rr": 0}
    sil_state = {}

    def emit_ffn(fb, tail_prefetch=False, convert=False):
        xn, ht, gub, dnb = fb["xn"], fb["ht"], fb["gu"], fb["dn"]
        cwb, cwT = fb["cwb"], fb["cwT"]
        silu_sb = fb["silu"]
        mcol, rcol, uneg = fb["mcol"], fb["rcol"], fb["uneg"]
        t1 = fb["t1"]
        t8a, lg8, ex8, ge8, srt8, cwbf = (fb["t8a"], fb["lg8"], fb["ex8"],
                                          fb["ge8"], fb["srt8"], fb["cwbf"])

        for wq in range(2):
            stats_done[wq] = stats_compute(wq, cols=(fb["mcol"], fb["rcol"], wq))
        stats_prefetch = [
            [(2, (fb["mcol"], fb["rcol"], 0)), (3, (fb["mcol"], fb["rcol"], 1))],
            [(4, (fb["mcol"], fb["rcol"], 0)), (5, (fb["mcol"], fb["rcol"], 1))],
            [(6, (fb["mcol"], fb["rcol"], 0)), (7, (fb["mcol"], fb["rcol"], 1))],
            [(w, None) for w in range(4)] if tail_prefetch else [],
        ]
        wb_n = {i: 0 for i in range(6)}
        staged_stats = []
        pend_stats = []
        for q in range(4):
            raw_w = convert and q == 0
            if convert and q == 1:
                for i in range(6):
                    P.wait("sync", dch["wb%d" % i], dch["wb%d" % i].n)
                    P.wait("gpsimd", dch["wb%d" % i], dch["wb%d" % i].n)
            tq0 = q * QT
            norm_mark = None
            for wq in range(2):
                w = q * 2 + wq
                vs = statsb_load(w)
                P.wait("vector", dch["sbst"], vs)
                P.wait("vector", c_pe, state["last_gu_mark"])
                P.wait("vector", c_pe, state.get("stats_mm_last"))
                for c in range(8):
                    P.put("vector", op_tt(t1[:], rv(c, tq0 + wq * W, W),
                                          statsb[:, 0:512], ALU.subtract))
                    norm_mark = P.put("vector", op_tt(
                        xn[:, c * QT + wq * W: c * QT + wq * W + W],
                        t1[:], statsb[:, 512:1024], ALU.mult), c_ve)
                st_state["statsb_war"] = norm_mark

            P.wait("vector", dch["col"], dch["col"].n)
            P.wait("vector", dch["cwout"], state["cwout_prev"])
            lgps = pb[4][:, 0:64]
            P.wait("tensor", c_ve, state.get("dn_done"))
            P.wait("tensor", c_sc, sil_state.get(1))
            for tt in range(8):
                for kc in range(8):
                    m = P.put("tensor", op_mm(lgps[:, tt * 8:(tt + 1) * 8],
                                              rv(kc, tq0 + tt * 128, 128),
                                              wr_sb[:, kc * 8:kc * 8 + 8],
                                              kc == 0, kc == 7),
                              c_pe if (kc == 7 and tt == 7) else None)
            P.wait("vector", c_pe, m)

            def bview(t):
                return t[:].unsqueeze(2).broadcast_to((128, 8, 8))

            def eview(t):
                return t[:].unsqueeze(1).broadcast_to((128, 8, 8))
            lg3 = lg8[:].rearrange("p (a b) -> p a b", b=8)
            P.put("vector", op_tt(lg3, lgps.rearrange("p (a b) -> p a b", b=8),
                                  bview(rcol), ALU.mult))
            P.put("vector", op_stt(uneg[:], mcol[:], -1.0, rcol[:], ALU.mult, ALU.mult))
            P.put("vector", op_tt(t8a[:].rearrange("p (a b) -> p a b", b=8), eview(sb8),
                                  bview(uneg), ALU.mult))
            lg_read = P.put("vector", op_tt(lg8[:], lg8[:], t8a[:], ALU.add), c_ve)
            P.put("vector", op_tt(lg3, lg3, eview(rb_b), ALU.add))
            for tt in range(8):
                P.put("vector", op_max8(srt8[:, tt * 8:(tt + 1) * 8],
                                        lg8[:, tt * 8:(tt + 1) * 8]), selfdep=(tt == 0))
            m2view = srt8[:, 1:64:8].unsqueeze(2).broadcast_to((128, 8, 8))
            P.put("vector", op_tt(ge8[:].rearrange("p (a b) -> p a b", b=8), lg3,
                                  m2view, ALU.is_ge))
            vvm = P.put("vector", op_copy(ex8[:], lg8[:]), c_ve)
            P.wait("scalar", c_ve, vvm)
            sc = P.put("scalar", op_act(ex8[:], ex8[:], AF.Exp, scale=f32(-1.0)), c_sc)
            P.wait("vector", c_sc, sc)
            P.put("vector", op_ts(ex8[:], ex8[:], 1.0, ALU.add))
            P.put("vector", op_recip(ex8[:], ex8[:]))
            vvm = P.put("vector", op_stt(cwbf[:], ex8[:], 0.5, ge8[:],
                                         ALU.mult, ALU.mult), c_ve)
            P.wait("tensor", c_ve, vvm)
            P.wait("tensor", c_ve, state.get("tp_read") or 0)
            for tt in range(8):
                m = P.put("tensor", op_tp(tp_big[0:8, tt * 128:(tt + 1) * 128],
                                          cwbf[:, tt * 8:(tt + 1) * 8], ident_bf[:]),
                          c_pe if tt == 7 else None)
            P.wait("vector", c_pe, m)
            tp_read = P.put("vector", op_copy(cwT[:], tp_big), c_ve)
            state["tp_read"] = tp_read
            state["col_read"] = lg_read
            P.wait("sync", c_ve, tp_read)
            vcw = P.put("sync", op_dma(bnc_cw_d[:], cwT[:]), dch["cwout"], 16)
            state["cwout_prev"] = vcw
            if dbg_on and q == 0:
                P.put("sync", op_dma(dbgcw_d, cwT[:]), dch["cwout"], 16)
                P.put("sync", op_dma(dbgcol_d[:, 0:8], mcol[:]), dch["cwout"], 16)
                P.put("sync", op_dma(dbgcol_d[:, 8:16], rcol[:]), dch["cwout"], 16)
                P.put("sync", op_dma(dbgcol_d[:, 16:24], uneg[:]), dch["cwout"], 16)
                state["cwout_prev"] = dch["cwout"].n

            cwb_ready = {}

            def issue_cwb(re):
                slot = re % 2
                ch = dch["cwb%d" % slot]
                P.wait("sync", dch["cwout"], vcw)
                if slot in cwb_use:
                    P.wait("sync", c_ve, cwb_use[slot])
                cwb_ready[re] = P.put("sync", op_dma(
                    cwb[:, slot * QT:(slot + 1) * QT],
                    bnc_cw_d[re:re + 1, :].to_broadcast((128, QT))), ch, 16)

            n_routed = max(0, n_exp - 1)
            if n_routed > 0:
                issue_cwb(0)

            h_last = norm_mark
            h_evs = {0: norm_mark, 1: norm_mark, 2: norm_mark}
            sil_mark = {}
            usb_mark = {}
            for ei in range(n_exp):
                routed = ei > 0
                re = ei - 1
                e = ei
                if routed and re + 1 < n_routed:
                    issue_cwb(re + 1)
                for half in range(2):
                    h_marks = {}
                    for icl in range(8):
                        ic = half * 8 + icl
                        gi = state["gu_rr"] % 3
                        state["gu_rr"] += 1
                        par = ic % 2
                        gps = pbp[par * 2]
                        ups = pbp[par * 2 + 1]
                        slot = gub[:, gi * 2048:(gi + 1) * 2048]
                        if raw_w:
                            ch = dch["gu%d" % gi]
                            P.wait("gpsimd", c_pe, gu_use.get(gi))
                            P.wait("gpsimd", dch["wb%d" % gi], wb_n[gi])
                            vgu = P.put("gpsimd", op_dma(slot, gu_d[e, ic]), ch, 16)
                            P.wait("sync", ch, vgu)
                            wb_n[gi] = P.put("sync", op_dma(gu_bf_d[e, ic], slot),
                                             dch["wb%d" % gi], 16)
                        elif ic % 2 == 0:
                            ch = dch["gus%d" % gi]
                            P.wait("sync", c_pe, gu_use.get(gi))
                            vgu = P.put("sync", op_dma(slot, gu_bf_d[e, ic]), ch, 16)
                        else:
                            ch = dch["gu%d" % gi]
                            P.wait("gpsimd", c_pe, gu_use.get(gi))
                            vgu = P.put("gpsimd", op_dma(slot, gu_bf_d[e, ic]), ch, 16)
                        P.wait("tensor", ch, vgu)
                        if par in sil_mark:
                            P.wait("tensor", c_sc, sil_mark[par])
                            P.wait("tensor", c_ve, usb_mark[par])
                        if icl == 0:
                            P.wait("tensor", c_ve, h_evs[0])
                            P.wait("tensor", c_ve, h_evs[1])
                        elif icl == 1:
                            P.wait("tensor", c_ve, h_evs[2])
                        for kc in range(8):
                            for w2 in range(2):
                                P.put("tensor", op_mm(
                                    gps[:, w2 * W:(w2 + 1) * W],
                                    gub[:, gi * 2048 + kc * 128: gi * 2048 + (kc + 1) * 128],
                                    xn[:, kc * QT + w2 * W: kc * QT + w2 * W + W],
                                    kc == 0, kc == 7))
                        for kc in range(8):
                            for w2 in range(2):
                                m = P.put("tensor", op_mm(
                                    ups[:, w2 * W:(w2 + 1) * W],
                                    gub[:, gi * 2048 + 1024 + kc * 128:
                                        gi * 2048 + 1024 + (kc + 1) * 128],
                                    xn[:, kc * QT + w2 * W: kc * QT + w2 * W + W],
                                    kc == 0, kc == 7),
                                    c_pe if (kc == 7 and w2 == 1) else None)
                        gu_use[gi] = m
                        state["last_gu_mark"] = m
                        P.wait("scalar", c_pe, m)
                        if par in usb_mark:
                            P.wait("scalar", c_ve, usb_mark[par])
                        sil_mark[par] = P.put("scalar", op_act(
                            silu_sb[:, par * 1024:(par + 1) * 1024], gps[:], AF.Silu), c_sc)
                        sil_state[par] = sil_mark[par]
                        P.wait("vector", c_sc, sil_mark[par])
                        if routed and ic == 0:
                            P.wait("vector", dch["cwb%d" % (re % 2)], cwb_ready[re])
                        hsl = ht[:, icl * QT: (icl + 1) * QT]
                        u_mark = P.put("vector", op_stt(
                            hsl, ups[:], 1.0 if routed else 0.5,
                            silu_sb[:, par * 1024:(par + 1) * 1024],
                            ALU.mult, ALU.mult), c_ve)
                        if routed:
                            sl = re % 2
                            u_mark = P.put("vector", op_tt(
                                hsl, hsl, cwb[:, sl * QT:(sl + 1) * QT], ALU.mult), c_ve)
                        usb_mark[par] = u_mark
                        h_marks[icl] = u_mark
                        if routed:
                            cwb_use[re % 2] = u_mark
                    h_ready = u_mark
                    dn_evac = {}
                    dn_rot = [0, 1, 2, 0, 1, 0, 1, 2]
                    for hc in range(8):
                        di = state["dn_rr"] % 3
                        state["dn_rr"] += 1
                        slot = dnb[:, di * 1024:(di + 1) * 1024]
                        if raw_w:
                            ch = dch["dn%d" % di]
                            P.wait("gpsimd", c_pe, dn_use.get(di))
                            P.wait("gpsimd", dch["wb%d" % (3 + di)], wb_n[3 + di])
                            vdn = P.put("gpsimd", op_dma(slot, dn_d[e, half, hc]), ch, 16)
                            P.wait("sync", ch, vdn)
                            wb_n[3 + di] = P.put("sync", op_dma(dn_bf_d[e, half, hc], slot),
                                                 dch["wb%d" % (3 + di)], 16)
                        elif hc % 2 == 0:
                            ch = dch["dns%d" % di]
                            P.wait("sync", c_pe, dn_use.get(di))
                            vdn = P.put("sync", op_dma(slot, dn_bf_d[e, half, hc]), ch, 16)
                        else:
                            ch = dch["dn%d" % di]
                            P.wait("gpsimd", c_pe, dn_use.get(di))
                            vdn = P.put("gpsimd", op_dma(slot, dn_bf_d[e, half, hc]), ch, 16)
                        P.wait("tensor", ch, vdn)
                        dpair = pbp[dn_rot[hc]]
                        if dn_rot[hc] in dn_evac:
                            P.wait("tensor", c_ve, dn_evac[dn_rot[hc]])
                        for icl2 in range(8):
                            if hc == 0:
                                P.wait("tensor", c_ve, h_marks[icl2])
                            for w2 in range(2):
                                m = P.put("tensor", op_mm(
                                    dpair[:, w2 * W:(w2 + 1) * W],
                                    dnb[:, di * 1024 + icl2 * 128: di * 1024 + (icl2 + 1) * 128],
                                    ht[:, icl2 * QT + w2 * W: icl2 * QT + w2 * W + W],
                                    icl2 == 0, icl2 == 7),
                                    c_pe if (icl2 == 7 and w2 == 1) else None)
                        dn_use[di] = m
                        P.wait("vector", c_pe, m)
                        dn_evac[dn_rot[hc]] = P.put("vector", op_tt(
                            rv(hc, tq0, QT), rv(hc, tq0, QT), dpair[:], ALU.add), c_ve)
                        if (ei == n_exp - 1 and half == 1 and hc == 1
                                and stats_prefetch):
                            pend_stats = stats_prefetch.pop(0)
                            if pend_stats:
                                w2f, colspec = pend_stats[0]
                                staged_stats = [(w2f, colspec,
                                                 stats_stage(w2f, xn), xn)]
                    h_last = dn_evac[2]
                    h_evs = dict(dn_evac)
                    state["dn_done"] = h_last
                    if ei == n_exp - 1 and half == 1 and staged_stats:
                        arenas = (xn, ht)
                        for i, (w2f, colspec) in enumerate(pend_stats):
                            if i > 0:
                                ar = arenas[i % 2]
                                P.wait("vector", c_pe, m)
                                P.wait("vector", c_pe, state.get("stats_mm_last"))
                                smk = stats_stage(w2f, ar)
                            else:
                                _, _, smk, ar = staged_stats[0]
                            stats_done[w2f] = stats_mms(
                                w2f, ar, smk, cols=colspec,
                                guard_ve=usb_mark.get(1))
                            state["stats_mm_last"] = st_state["pe7"]
                        staged_stats = []
                        pend_stats = []
            st_state["pb56_war"] = h_last

    ffn_alloc_n = [0]

    def alloc_ffn(st):
        ffn_alloc_n[0] += 1
        sfx = str(ffn_alloc_n[0])
        return dict(
            xn=sb("xn" + sfx, [128, 8 * QT], BF16, st),
            ht=sb("ht" + sfx, [128, 8 * QT], BF16, st),
            gu=sb("gub" + sfx, [128, 3 * 2048], BF16, st),
            dn=sb("dnb" + sfx, [128, 3 * 1024], BF16, st),
            cwb=sb("cwbb" + sfx, [128, 2 * QT], BF16, st),
            cwT=sb("cwT" + sfx, [8, 1024], BF16, st),
            silu=sb("silu_sb" + sfx, [128, 2048], BF16, st),
            mcol=sb("mcol" + sfx, [128, 8], F32, st),
            rcol=sb("rcol" + sfx, [128, 8], F32, st),
            uneg=sb("uneg" + sfx, [128, 8], F32, st),
            t8a=sb("t8a" + sfx, [128, 64], F32, st),
            lg8=sb("lg8" + sfx, [128, 64], F32, st),
            ex8=sb("ex8" + sfx, [128, 64], F32, st),
            ge8=sb("ge8" + sfx, [128, 64], F32, st),
            srt8=sb("srt8" + sfx, [128, 64], F32, st),
            cwbf=sb("cwbf" + sfx, [128, 64], BF16, st),
            t1=sb("t1" + sfx, [128, 512], F32, st),
        )

    io_st.close()

    if do_ffn0:
        ffn_st = ExitStack()
        fb = alloc_ffn(ffn_st)
        engine_guard("gpsimd")
        dma_quiesce("gpsimd")
        emit_ffn(fb, tail_prefetch=do_attn, convert=True)
        ffn_st.close()

    if do_attn:
        ast = ExitStack()
        rx = sb("rx", [128, 8 * S], BF16, ast)

        engine_guard("vector")
        engine_guard("gpsimd")
        engine_guard("sync")
        dma_quiesce("sync")
        dma_quiesce("gpsimd")

        for w_ in range(4, 8):
            stats_done.pop(w_, None)
        qkv_use = None
        wo_use = None
        state_att = {}
        rope_war = None
        oproj_done = None
        prev_att15 = None
        prev_attcopy = None
        prev_recip = None
        for b in range(BPC):
            bt0 = b * S
            stats_st = ExitStack()
            cs = sb(f"cs{b}", [128, 1024], F32, stats_st)
            nf = sb(f"nf{b}", [128, 1024], F32, stats_st)
            tsc = sb(f"tsc{b}", [128, 1024], F32, stats_st)
            P.wait("sync", c_pe, c_pe.n)
            P.wait("sync", c_ve, c_ve.n)
            P.wait("vector", c_pe, c_pe.n)
            P.wait("vector", c_sc, c_sc.n)
            for wl in range(4):
                w = b * 4 + wl
                st_state["pb56_war"] = oproj_done or st_state["pb56_war"]
                if w not in stats_done:
                    stats_done[w] = stats_compute(w)
                vs = statsb_load(w)
                stats_done.pop(w)
                P.wait("vector", dch["sbst"], vs)
                if rope_war is not None:
                    P.wait("vector", c_pe, rope_war)
                    rope_war = None
                for cp in range(4):
                    P.put("sync", op_dma(cs[:, 0:512],
                                         cos_d[cp, :, wl * W:(wl + 1) * W]), dch["cs"], 16)
                    vcs = P.put("sync", op_dma(cs[:, 512:1024],
                                               sin_d[cp, :, wl * W:(wl + 1) * W]), dch["cs"], 16)
                    P.wait("vector", dch["cs"], vcs)
                    na = nf[:, 0:512]
                    nb2 = nf[:, 512:1024]
                    P.put("vector", op_tt(na, rv(cp, bt0 + wl * W, W),
                                          statsb[:, 0:512], ALU.subtract))
                    P.put("vector", op_tt(na, na, statsb[:, 512:1024], ALU.mult))
                    P.put("vector", op_tt(nb2, rv(cp + 4, bt0 + wl * W, W),
                                          statsb[:, 0:512], ALU.subtract))
                    P.put("vector", op_tt(nb2, nb2, statsb[:, 512:1024], ALU.mult))
                    u1 = tsc[:, 0:512]
                    u2 = tsc[:, 512:1024]
                    P.put("vector", op_tt(u1, na, cs[:, 0:512], ALU.mult))
                    P.put("vector", op_tt(u2, nb2, cs[:, 512:1024], ALU.mult))
                    P.put("vector", op_tt(rx[:, cp * S + wl * W: cp * S + wl * W + W],
                                          u1, u2, ALU.subtract))
                    P.put("vector", op_tt(u1, na, cs[:, 512:1024], ALU.mult))
                    P.put("vector", op_tt(u2, nb2, cs[:, 0:512], ALU.mult))
                    sb_war = P.put("vector", op_tt(
                        rx[:, (cp + 4) * S + wl * W: (cp + 4) * S + wl * W + W],
                        u1, u2, ALU.add), c_ve)
                    P.wait("sync", c_ve, sb_war)
                st_state["statsb_war"] = sb_war
            stats_st.close()

            hp_st = ExitStack()
            k2 = sb(f"k2_{b}", [128, S], BF16, hp_st)
            v2 = sb(f"v2_{b}", [128, 16 * 130], BF16, hp_st)
            qb = sb(f"qb{b}", [128, 1024], BF16, hp_st)
            qkvb = sb(f"qkvb{b}", [128, 3072], BF16, hp_st)
            wob = sb(f"wob{b}", [64, 2048], BF16, hp_st)
            psb = sb(f"psb{b}", [128, 4096], BF16, hp_st)
            a0 = psb[0:64, 0:1024]
            a1 = psb[0:64, 2048:3072]
            rcp = sb(f"rcp{b}", [65, 1024], BF16, hp_st)
            P.wait("gpsimd", dch["cs"], dch["cs"].n)
            P.wait("gpsimd", c_ve, c_ve.n)
            P.wait("vector", dch["cs"], dch["cs"].n)
            mset = None
            for kt in range(16):
                P.put("vector", op_memset(v2[:, kt * 130 + 64: kt * 130 + 65], 1.0))
                mset = P.put("vector", op_memset(v2[:, kt * 130 + 129: kt * 130 + 130], 1.0), c_ve)
            v2_ready = mset

            for hp in range(8):
                if qkv_use is not None:
                    P.wait("gpsimd", c_pe, qkv_use)
                vq = P.put("gpsimd", op_dma(qkvb[:], qkv_d[hp]), dch["qkv"], 16)
                if wo_use is not None:
                    P.wait("gpsimd", c_pe, wo_use)
                vw = P.put("gpsimd", op_dma(wob[:], wo_d[hp]), dch["wo"], 16)
                P.wait("tensor", dch["qkv"], vq)
                if prev_attcopy is not None:
                    P.wait("tensor", c_sc, prev_attcopy)
                    P.wait("tensor", c_ve, prev_recip)
                kv_evac = {}
                for wl in range(4):
                    bank = pb[wl % 2]
                    if wl >= 2:
                        P.wait("tensor", c_ve, kv_evac[wl - 2])
                    for kc in range(8):
                        m = P.put("tensor", op_mm(
                            bank[:], qkvb[:, 1024 + kc * 128: 1024 + (kc + 1) * 128],
                            rx[:, kc * S + wl * W: kc * S + wl * W + W],
                            kc == 0, kc == 7), c_pe if kc == 7 else None)
                    P.wait("vector", c_pe, m)
                    kv_evac[wl] = P.put("vector", op_ts(
                        k2[:, wl * W:(wl + 1) * W], bank[:], bk_c[:, hp:hp + 1], ALU.add), c_ve)
                if hp == 0:
                    P.wait("tensor", c_ve, v2_ready)
                vbanks = (pb[2], pb[3], pb[6], pb[7])
                P.wait("tensor", c_ve, oproj_done)
                for vt in range(16):
                    bank = vbanks[vt % 4]
                    if vt >= 4:
                        P.wait("tensor", c_ve, kv_evac[("v", vt - 4)])
                    for kc in range(8):
                        m = P.put("tensor", op_mm(
                            bank[:, 0:128], rx[:, kc * S + vt * 128: kc * S + (vt + 1) * 128],
                            qkvb[:, 2048 + kc * 128: 2048 + (kc + 1) * 128],
                            kc == 0, kc == 7), c_pe if kc == 7 else None)
                    P.wait("vector", c_pe, m)
                    P.put("vector", op_tt(v2[:, vt * 130: vt * 130 + 64], bank[:, 0:64],
                                          bv_b[:, hp * 128: hp * 128 + 64], ALU.add))
                    kv_evac[("v", vt)] = P.put("vector", op_tt(
                        v2[:, vt * 130 + 65: vt * 130 + 129], bank[:, 64:128],
                        bv_b[:, hp * 128 + 64: hp * 128 + 128], ALU.add), c_ve)
                P.wait("tensor", c_ve, kv_evac[("v", 15)])
                P.wait("tensor", c_ve, kv_evac[3])

                exp_prev_h = None
                def q_build(wlp, exp_guard=None):
                    nonlocal qkv_use
                    for w2 in range(2):
                        wl = wlp * 2 + w2
                        bank = pb[w2]
                        if exp_guard is not None:
                            P.wait("tensor", c_sc, exp_guard)
                        elif w2 == 0:
                            P.wait("tensor", c_ve, kv_evac.get(3))
                        for kc in range(8):
                            m = P.put("tensor", op_mm(
                                bank[:], qkvb[:, kc * 128:(kc + 1) * 128],
                                rx[:, kc * S + wl * W: kc * S + wl * W + W],
                                kc == 0, kc == 7), c_pe if kc == 7 else None)
                        P.wait("vector", c_pe, m)
                        kv_evac[("q", w2)] = P.put("vector", op_ts(
                            qb[:, w2 * W:(w2 + 1) * W], bank[:],
                            bq_c[:, hp:hp + 1], ALU.add), c_ve)
                    qkv_use = m

                prev_exp = state_att.get("prev_exp", {})
                for wlp in range(2):
                    if wlp == 0:
                        q_build(0)
                    P.wait("tensor", c_ve, kv_evac[("q", 1)])

                    attcopy_h = {}
                    recip_h = {}
                    attn_m = {}
                    pe_S = {}
                    pe_att = {}
                    exp_m = {}

                    def s_pair(h):
                        return pbp[0] if h == 0 else pbp[2]

                    def att_banks(h):
                        return (pb[2], pb[3]) if h == 0 else (pb[6], pb[7])

                    def emit_S(h, kt):
                        sp = s_pair(h)
                        if kt >= 1:
                            P.wait("tensor", c_sc, exp_m[(h, kt - 1)])
                        else:
                            if (h, 15) in prev_exp:
                                P.wait("tensor", c_sc, prev_exp[(h, 15)])
                            P.wait("tensor", c_ve, oproj_done)
                        P.put("tensor", op_mm(
                            sp[:, 0:512], k2[h * 64:(h + 1) * 64, kt * 128:(kt + 1) * 128],
                            qb[h * 64:(h + 1) * 64, 0:512], True, True))
                        pe_S[(h, kt)] = P.put("tensor", op_mm(
                            sp[:, 512:1024], k2[h * 64:(h + 1) * 64, kt * 128:(kt + 1) * 128],
                            qb[h * 64:(h + 1) * 64, 512:1024], True, True), c_pe)

                    def emit_exp(h, kt):
                        sp = s_pair(h)
                        P.wait("scalar", c_pe, pe_S[(h, kt)])
                        if kt >= 2:
                            P.wait("scalar", c_pe, pe_att[(h, kt - 2)])
                        else:
                            if state_att.get("patt%d" % h) is not None:
                                P.wait("scalar", c_pe, state_att["patt%d" % h])
                            P.wait("scalar", c_pe, wo_use)
                        sl0 = (h * 2 + kt % 2) * 1024
                        exp_m[(h, kt)] = P.put("scalar", op_act(
                            psb[:, sl0: sl0 + 1024], sp[:], AF.Exp,
                            bias=f32(-20.0), scale=f32(0.125)), c_sc)

                    def emit_att(h, kt):
                        ab = att_banks(h)
                        P.wait("tensor", c_sc, exp_m[(h, kt)])
                        if kt == 0:
                            P.wait("tensor", c_sc, state_att.get("copy%d" % h))
                            P.wait("tensor", c_ve, state_att.get("rdTT%d" % h))
                            P.wait("tensor", c_ve, oproj_done)
                        sl0 = (h * 2 + kt % 2) * 1024
                        for w2 in range(2):
                            mm_m = P.put("tensor", op_mm(
                                ab[w2][0:65, :],
                                v2[:, kt * 130 + h * 65: kt * 130 + (h + 1) * 65],
                                psb[:, sl0 + w2 * 512: sl0 + (w2 + 1) * 512],
                                kt == 0, kt == 15),
                                c_pe if w2 == 1 else None)
                        pe_att[(h, kt)] = mm_m

                    for h in range(2):
                        emit_S(h, 0)
                        emit_exp(h, 0)
                    for h in range(2):
                        emit_S(h, 1)
                        emit_exp(h, 1)
                    for kt in range(16):
                        for h in range(2):
                            emit_att(h, kt)
                        if kt + 2 < 16:
                            for h in range(2):
                                emit_S(h, kt + 2)
                                emit_exp(h, kt + 2)

                    def finish_a(h):
                        ah = (a0, a1)[h]
                        ab = att_banks(h)
                        P.wait("scalar", c_pe, pe_att[(h, 15)])
                        P.wait("scalar", c_pe, wo_use)
                        for w2 in range(2):
                            attcopy_h[h] = P.put("scalar", op_scopy(
                                ah[0:64, w2 * 512:(w2 + 1) * 512],
                                ab[w2][0:64, :]), c_sc, selfdep=(w2 == 0))
                        state_att["copy%d" % h] = attcopy_h[h]
                        P.wait("vector", c_pe, pe_att[(h, 15)])
                        P.wait("vector", c_sc, attcopy_h[h])
                        P.wait("vector", c_pe, state_att.get("bc_rd"))
                        for w2 in range(2):
                            recip_h[h] = P.put("vector", op_recip(
                                rcp[64:65, w2 * 512:(w2 + 1) * 512],
                                ab[w2][64:65, :], nc), c_ve, selfdep=(w2 == 0))

                    def finish_b(h):
                        ah = (a0, a1)[h]
                        ab = att_banks(h)
                        P.wait("tensor", c_ve, recip_h[h])
                        bcms = []
                        for w2 in range(2):
                            bcms.append(P.put("tensor", op_mm(
                                ab[w2][0:64, :], ones_bf[64:65, 0:64],
                                rcp[64:65, w2 * 512:(w2 + 1) * 512], True, True), c_pe))
                        state_att["bc_rd"] = bcms[1]
                        for w2 in range(2):
                            P.wait("vector", c_pe, bcms[w2])
                            attn_m[(h, w2)] = P.put("vector", op_tt(
                                ah[0:64, w2 * 512:(w2 + 1) * 512],
                                ah[0:64, w2 * 512:(w2 + 1) * 512],
                                ab[w2][0:64, :], ALU.mult), c_ve)
                        state_att["rdTT%d" % h] = attn_m[(h, 1)]

                    finish_a(0)
                    finish_a(1)
                    if wlp == 0:
                        q_build(1, exp_guard=exp_m[(0, 14)])
                    finish_b(0)
                    finish_b(1)
                    state_att["patt0"] = pe_att[(0, 14)]
                    state_att["patt1"] = pe_att[(1, 14)]
                    prev_exp = {(0, 15): exp_m[(0, 15)], (1, 15): exp_m[(1, 15)]}
                    state_att["prev_exp"] = prev_exp
                    prev_att15 = pe_att[(1, 15)]
                    prev_attcopy = attcopy_h[1]
                    prev_recip = max(recip_h[0], recip_h[1])

                    bank_ids = [2, 3, 4, 5]
                    oproj_evac = {}
                    first_op = True
                    for w2 in range(2):
                        wl = wlp * 2 + w2
                        for fc in range(8):
                            bid = bank_ids[(w2 * 8 + fc) % 4]
                            bank = pb[bid]
                            if first_op:
                                P.wait("tensor", c_sc, exp_m[(1, 15)])
                                P.wait("tensor", c_ve, attn_m[(1, 1)])
                                first_op = False
                            if bid in oproj_evac:
                                P.wait("tensor", c_ve, oproj_evac[bid])
                            P.put("tensor", op_mm(
                                bank[:], wob[0:64, fc * 128:(fc + 1) * 128],
                                a0[0:64, w2 * 512:(w2 + 1) * 512], True, False))
                            m = P.put("tensor", op_mm(
                                bank[:], wob[0:64, 1024 + fc * 128: 1024 + (fc + 1) * 128],
                                a1[0:64, w2 * 512:(w2 + 1) * 512], False, True), c_pe)
                            P.wait("vector", c_pe, m)
                            oproj_evac[bid] = P.put("vector", op_tt(
                                rv(fc, bt0 + wl * W, W), rv(fc, bt0 + wl * W, W),
                                bank[:], ALU.add), c_ve)
                    wo_use = m
                    oproj_done = oproj_evac[bank_ids[(1 * 8 + 7) % 4]]
            hp_st.close()
            for fc in range(8):
                bom = P.put("vector", op_ts(rv(fc, bt0, S), rv(fc, bt0, S),
                                            bo_c[:, fc:fc + 1], ALU.add), c_ve)
            oproj_done = bom
            rope_war = qkv_use
        st_state["pb56_war"] = oproj_done
        ast.close()

    if do_ffn1:
        ffn_st = ExitStack()
        fb = alloc_ffn(ffn_st)
        engine_guard("gpsimd")
        engine_guard("vector")
        dma_quiesce("gpsimd")
        dma_quiesce("sync")
        gu_use.clear()
        dn_use.clear()
        cwb_use.clear()
        gu_use[0] = c_pe.n
        gu_use[1] = c_pe.n
        dn_use[0] = c_pe.n
        dn_use[1] = c_pe.n
        cwb_use[0] = c_ve.n
        cwb_use[1] = c_ve.n
        emit_ffn(fb)
        ffn_st.close()

    io_st2 = ExitStack()
    yio = sb("yio", [128, 2048], F32, io_st2)
    engine_guard("tensor")
    st_marks = {}
    cp_marks = {}
    for tt in range(32):
        sl = tt % 2
        if tt >= 2:
            P.wait("vector", dch["yout%d" % sl], st_marks[tt - 2])
        vm = None
        for c in range(8):
            gidx = tt * 8 + c
            bank = pb[c % 4]
            if gidx - 4 in cp_marks:
                P.wait("tensor", c_ve, cp_marks[gidx - 4])
            m = P.put("tensor", op_tp(bank[:, 0:128], rv(c, tt * 128, 128), ident32[:]), c_pe)
            P.wait("vector", c_pe, m)
            vm = P.put("vector", op_copy(yio[:, sl * 1024 + c * 128: sl * 1024 + (c + 1) * 128],
                                         bank[:, 0:128]), c_ve)
            cp_marks[gidx] = vm
        P.wait("sync", c_ve, vm)
        st_marks[tt] = P.put("sync", op_dma(y_d[tt * 128:(tt + 1) * 128, :],
                                            yio[:, sl * 1024:(sl + 1) * 1024]),
                             dch["yout%d" % sl], 16)
    dma_quiesce("sync")
    io_st2.close()

    with nc.Block() as blk:
        @blk.sync
        def _(e):
            for fop in P.ops["sync"]:
                fop(e)

        @blk.gpsimd
        def _(e):
            for fop in P.ops["gpsimd"]:
                fop(e)

        @blk.tensor
        def _(e):
            for fop in P.ops["tensor"]:
                fop(e)

        @blk.vector
        def _(e):
            for fop in P.ops["vector"]:
                fop(e)

        @blk.scalar
        def _(e):
            for fop in P.ops["scalar"]:
                fop(e)

    es.close()
    return nc



def _host_prep(inputs):
    gu_all = np.stack(
        [_prep_gu(np.asarray(inputs["L0_ffn_Wg_s"]), np.asarray(inputs["L0_ffn_Wu_s"]))] +
        [_prep_gu(np.asarray(inputs["L0_ffn_Wg"])[e], np.asarray(inputs["L0_ffn_Wu"])[e])
         for e in range(NE)])
    dn_all = np.stack(
        [_prep_dn(np.asarray(inputs["L0_ffn_Wd_s"]))] +
        [_prep_dn(np.asarray(inputs["L0_ffn_Wd"])[e]) for e in range(NE)])
    cos_h, sin_h = _prep_rope()
    return dict(
        gu=gu_all, dn=dn_all,
        qkv=_prep_qkv(np.asarray(inputs["L0_attn_Wq"]), np.asarray(inputs["L0_attn_Wk"]),
                      np.asarray(inputs["L0_attn_Wv"])),
        wo=_prep_wo(np.asarray(inputs["L0_attn_Wo"])),
        wr=np.ascontiguousarray(np.asarray(inputs["L0_ffn_Wr"]).T
                                .reshape(8, 128, 8).transpose(1, 0, 2).reshape(128, 64)),
        rbb=np.ascontiguousarray(np.broadcast_to(
            np.asarray(inputs["L0_ffn_rb"], dtype=f32), (128, 8))),
        bq=_prep_bias_cols(np.asarray(inputs["L0_attn_bq"])),
        bk=_prep_bias_cols(np.asarray(inputs["L0_attn_bk"])),
        bo=_prep_bias_cols(np.asarray(inputs["L0_attn_bo"])),
        bvb=np.ascontiguousarray(np.broadcast_to(
            np.asarray(inputs["L0_attn_bv"], dtype=f32), (128, D))),
        cosd=cos_h, sind=sin_h,
        ident=np.eye(128, dtype=f32),
        onesd=np.ones((128, 128), dtype=f32),
    )


def kernel(**inputs):
    xs = np.asarray(inputs["L0_attn_x"], dtype=np.float32)
    shared = _host_prep(inputs)
    nc = build_program()
    in_maps = []
    for c in range(NCORE):
        m = dict(shared)
        m["x"] = np.ascontiguousarray(xs[2 * c:2 * c + 2].reshape(T, D))
        in_maps.append(m)
    res = bass_utils.run_bass_kernel_spmd(nc, in_maps, core_ids=list(range(NCORE)))
    out = np.concatenate([np.asarray(res.results[c]["y"]).reshape(BPC, S, D)
                          for c in range(NCORE)], axis=0)
    return out


# version 16
# speedup vs baseline: 1.1341x; 1.1341x over previous
import math
from contextlib import ExitStack

import numpy as np

import concourse.bass as bass
import concourse.mybir as mybir
import concourse.bass_utils as bass_utils

F32 = mybir.dt.float32
BF16 = mybir.dt.bfloat16
AF = mybir.ActivationFunctionType
ALU = mybir.AluOpType

D = 1024
I = 2048
NH = 16
HD = 64
S = 2048
BPC = 2
T = BPC * S
NE = 8
NCORE = 8
QT = 1024
W = 512

f32 = np.float32



def _prep_gu(Wg, Wu):
    g = Wg.reshape(16, 128, 8, 128).transpose(0, 3, 2, 1)
    u = Wu.reshape(16, 128, 8, 128).transpose(0, 3, 2, 1)
    gu = np.stack([g, u], axis=2)
    return np.ascontiguousarray(gu.reshape(16, 128, 2048))


def _prep_dn(Wd):
    d = Wd.reshape(8, 128, 16, 128)
    d = d.transpose(2, 0, 3, 1)
    d = d.reshape(2, 8, 8, 128, 128)
    d = d.transpose(0, 2, 3, 1, 4)
    return np.ascontiguousarray(d.reshape(2, 8, 128, 1024))


def _prep_qkv(Wq, Wk, Wv):
    def t(Wm):
        return Wm.reshape(8, 128, 8, 128).transpose(0, 3, 2, 1)
    a = np.stack([t(Wq), t(Wk), t(Wv)], axis=2)
    return np.ascontiguousarray(a.reshape(8, 128, 3072))


def _prep_wo(Wo):
    a = Wo.reshape(8, 128, 8, 128)
    a = a.transpose(2, 3, 0, 1)
    a = a.reshape(8, 2, 64, 8, 128)
    a = a.transpose(0, 2, 1, 3, 4)
    return np.ascontiguousarray(a.reshape(8, 64, 2048))


def _prep_bias_cols(b):
    return np.ascontiguousarray(b.reshape(8, 128).T)


def _prep_rope():
    a = np.arange(0, D, 2, dtype=f32) * f32(-math.log(10000.0) / D)
    freq = np.exp(a).astype(f32)
    ang = (np.arange(S, dtype=f32)[:, None] * freq[None, :]).astype(f32)
    cos = np.cos(ang).astype(f32).T
    sin = np.sin(ang).astype(f32).T
    return (np.ascontiguousarray(cos.reshape(4, 128, S)),
            np.ascontiguousarray(sin.reshape(4, 128, S)))



ENGS = ("sync", "gpsimd", "tensor", "vector", "scalar")


class Chan:
    __slots__ = ("sem", "n")

    def __init__(self):
        self.sem = None
        self.n = 0


class Prog:

    def __init__(self):
        self.ops = {e: [] for e in ENGS}
        self.auto = {}

    def put(self, eng, fop, ch=None, k=1, selfdep=True):
        if eng in ("vector", "scalar"):
            auto = self.auto[eng]
            if selfdep:
                self.wait(eng, auto, auto.n)
            ch = auto
            k = 1
        if ch is None:
            self.ops[eng].append(fop)
            return None
        ch.n += k
        sem, kk = ch.sem, k

        def g(e, fop=fop, sem=sem, kk=kk):
            fop(e).then_inc(sem, kk)
        self.ops[eng].append(g)
        return ch.n

    def wait(self, eng, ch, val):
        if val is None or val <= 0:
            return
        sem = ch.sem

        def g(e, sem=sem, val=val):
            e.wait_ge(sem, val)
        self.ops[eng].append(g)


def op_dma(dst, src, **kw):
    def fop(e):
        return e.dma_start(dst, src, **kw)
    return fop


def op_mm(out, lhsT, rhs, start, stop):
    def fop(e):
        return e.matmul(out, lhsT, rhs, start=start, stop=stop)
    return fop


def op_tp(out, in_, ident):
    def fop(e):
        return e.transpose(out, in_, ident)
    return fop


def op_act(out, in_, func, bias=None, scale=None):
    kw = {}
    if bias is not None:
        kw["bias"] = bias
    if scale is not None:
        kw["scale"] = scale

    def fop(e):
        return e.activation(out, in_, func, **kw)
    return fop


def op_tt(out, in0, in1, op):
    def fop(e):
        return e.tensor_tensor(out, in0, in1, op)
    return fop


def op_ts(out, in0, s1, op0):
    def fop(e):
        return e.tensor_scalar(out, in0, s1, None, op0)
    return fop


def op_stt(out, in0, s, in1, op0, op1):
    def fop(e):
        return e.scalar_tensor_tensor(out, in0, s, in1, op0, op1)
    return fop


def op_copy(out, in_):
    def fop(e):
        return e.tensor_copy(out, in_)
    return fop


def op_scopy(out, in_):
    def fop(e):
        return e.copy(out, in_)
    return fop


def op_max8(out, in_):
    def fop(e):
        return e.max(out, in_)
    return fop


def op_recip(out, in_, nc=None):
    def fop(e):
        if nc is not None:
            with nc.allow_low_precision(reason="softmax 1/denom -> bf16 bcast row"):
                return e.reciprocal(out, in_)
        return e.reciprocal(out, in_)
    return fop


def op_memset(ap_, v):
    def fop(e):
        return e.memset(ap_, v)
    return fop



def build_program(cfg=None):
    cfg = dict(cfg or {})
    do_ffn0 = cfg.get("ffn0", True)
    do_attn = cfg.get("attn", True)
    do_ffn1 = cfg.get("ffn1", True)
    n_exp = cfg.get("n_experts", 1 + NE)

    nc = bass.Bass("TRN2", target_bir_lowering=False)

    def dram(name, shape, dt, kind=None):
        if kind is None:
            return nc.dram_tensor(name, shape, dt).ap()
        return nc.dram_tensor(name, shape, dt, kind=kind).ap()

    x_d = dram("x", [T, D], F32, "ExternalInput")
    y_d = dram("y", [T, D], F32, "ExternalOutput")
    gu_d = dram("gu", [9, 16, 128, 2048], F32, "ExternalInput")
    dn_d = dram("dn", [9, 2, 8, 128, 1024], F32, "ExternalInput")
    qkv_d = dram("qkv", [8, 128, 3072], F32, "ExternalInput")
    wo_d = dram("wo", [8, 64, 2048], F32, "ExternalInput")
    wr_d = dram("wr", [128, 64], F32, "ExternalInput")
    rbb_d = dram("rbb", [128, 8], F32, "ExternalInput")
    bq_d = dram("bq", [128, 8], F32, "ExternalInput")
    bk_d = dram("bk", [128, 8], F32, "ExternalInput")
    bv_d = dram("bvb", [128, D], F32, "ExternalInput")
    bo_d = dram("bo", [128, 8], F32, "ExternalInput")
    cos_d = dram("cosd", [4, 128, S], F32, "ExternalInput")
    sin_d = dram("sind", [4, 128, S], F32, "ExternalInput")
    id_d = dram("ident", [128, 128], F32, "ExternalInput")
    ones_d = dram("onesd", [128, 128], F32, "ExternalInput")
    bnc_st_d = dram("bnc_st", [8, 1024], BF16)
    bnc_cw_d = dram("bnc_cw", [8, 1024], BF16)
    gu_bf_d = dram("gu_bf", [9, 16, 128, 2048], BF16)
    dn_bf_d = dram("dn_bf", [9, 2, 8, 128, 1024], BF16)
    dbg_on = cfg.get("dbg", False)
    if dbg_on:
        dbgcw_d = dram("dbgcw", [8, 1024], BF16, "ExternalOutput")
        dbgcol_d = dram("dbgcol", [128, 24], F32, "ExternalOutput")

    P = Prog()
    es = ExitStack()

    def sb(name, shape, dt, st=None):
        return (st or es).enter_context(nc.sbuf_tensor(name, shape, dt))

    r = sb("r", [128, 8 * T], F32)
    ident32 = sb("ident32", [128, 128], F32)
    ident_bf = sb("ident_bf", [128, 128], BF16)
    ones_bf = sb("ones_bf", [128, 128], BF16)
    ones32 = sb("ones32", [128, 128], F32)
    onescol_bf = sb("onescol_bf", [128, 1], BF16)
    wr_sb = sb("wr_sb", [128, 64], F32)
    rb_b = sb("rb_b", [128, 8], F32)
    sb8 = sb("sb8", [128, 8], F32)
    s8row = sb("s8row", [1, 8], F32)
    bq_c = sb("bq_c", [128, 8], F32)
    bk_c = sb("bk_c", [128, 8], F32)
    bo_c = sb("bo_c", [128, 8], F32)
    bv_b = sb("bv_b", [128, D], BF16)
    stats_row = sb("stats_row", [1, 1024], BF16)
    srow = sb("srow", [1, 512], F32)
    statsb = sb("statsb", [128, 1024], BF16)
    xbt = sb("xbt", [128, 1024], BF16)
    sqt = sb("sqt", [128, 1024], BF16)

    pbp = [es.enter_context(nc.psum_tensor(f"pbp{i}", [128, 1024], F32)) for i in range(4)]

    def pbv(i):
        return pbp[i // 2][:, (i % 2) * 512:(i % 2) * 512 + 512]
    pb = [pbv(i) for i in range(8)]
    tp_big = pbp[3][0:8, 0:512].bitcast(BF16)

    c_pe, c_ve, c_sc = Chan(), Chan(), Chan()
    dch = {k: Chan() for k in
           ("gu0", "gu1", "gu2", "dn0", "dn1", "dn2",
            "gus0", "gus1", "gus2", "dns0", "dns1", "dns2",
            "wb0", "wb1", "wb2", "wb3", "wb4", "wb5",
            "qkv", "wo", "cwb0", "cwb1",
            "bout", "sbst", "col", "cs", "xin0", "xin1", "yout0", "yout1",
            "const", "constg", "cwout")}
    for i, ch in enumerate([c_pe, c_ve, c_sc] + list(dch.values())):
        ch.sem = es.enter_context(nc.semaphore(f"sem{i}"))
    P.auto["vector"] = c_ve
    P.auto["scalar"] = c_sc

    def rv(c, t0, n):
        return r[:, c * T + t0: c * T + t0 + n]

    def dma_quiesce(eng):
        for ch in dch.values():
            P.wait(eng, ch, ch.n)

    def engine_guard(eng):
        P.wait(eng, c_pe, c_pe.n)
        P.wait(eng, c_ve, c_ve.n)
        P.wait(eng, c_sc, c_sc.n)

    P.put("sync", op_dma(ident32[:], id_d), dch["const"], 16)
    P.put("sync", op_dma(ones32[:], ones_d), dch["const"], 16)
    P.put("sync", op_dma(wr_sb[:], wr_d), dch["const"], 16)
    P.put("sync", op_dma(rb_b[:], rbb_d), dch["const"], 16)
    P.put("sync", op_dma(bq_c[:], bq_d), dch["const"], 16)
    P.put("sync", op_dma(bk_c[:], bk_d), dch["const"], 16)
    P.put("sync", op_dma(bo_c[:], bo_d), dch["const"], 16)
    P.put("gpsimd", op_dma(ones_bf[:], ones_d), dch["constg"], 16)
    P.put("gpsimd", op_dma(bv_b[:], bv_d), dch["constg"], 16)
    P.put("gpsimd", op_dma(ident_bf[:], id_d), dch["constg"], 16)
    P.put("gpsimd", op_dma(onescol_bf[:], ones_d[:, 0:1],
          allow_slow_non_contiguous=True), dch["constg"], 16)
    n_const = dch["const"].n
    n_constg = dch["constg"].n

    P.wait("tensor", dch["const"], n_const)
    P.wait("tensor", dch["constg"], n_constg)
    for kc in range(8):
        v_pe = P.put("tensor", op_mm(pb[1][0:1, 0:8], ones32[:, 0:1],
                                     wr_sb[:, kc * 8:kc * 8 + 8], kc == 0, kc == 7),
                     c_pe if kc == 7 else None)
    P.wait("vector", c_pe, v_pe)
    vvm = P.put("vector", op_copy(s8row[:], pb[1][0:1, 0:8]), c_ve)
    P.wait("tensor", c_ve, vvm)
    v_pe = P.put("tensor", op_mm(pb[2][:, 0:8], ones32[0:1, :], s8row[0:1, :],
                                 True, True), c_pe)
    P.wait("vector", c_pe, v_pe)
    P.put("vector", op_copy(sb8[:], pb[2][:, 0:8]), c_ve)

    io_st = ExitStack()
    xio = sb("xio", [128, 2048], F32, io_st)
    P.wait("tensor", c_ve, c_ve.n)
    pe_io = {}
    ve_io = {}
    for tt in range(32):
        sl = tt % 2
        if tt >= 2:
            P.wait("sync", c_pe, pe_io[tt - 2])
        xch = dch["xin%d" % sl]
        vx = P.put("sync", op_dma(xio[:, sl * 1024:(sl + 1) * 1024],
                                  x_d[tt * 128:(tt + 1) * 128, :]), xch, 16)
        P.wait("tensor", xch, vx)
        for c in range(8):
            bank = pb[c % 4]
            prev = (tt, c - 4) if c >= 4 else (tt - 1, c + 4)
            if prev in ve_io:
                P.wait("tensor", c_ve, ve_io[prev])
            m = P.put("tensor", op_tp(bank[:, 0:128],
                                      xio[:, sl * 1024 + c * 128: sl * 1024 + (c + 1) * 128],
                                      ident32[:]), c_pe)
            if c == 7:
                pe_io[tt] = m
            P.wait("vector", c_pe, m)
            ve_io[(tt, c)] = P.put("vector", op_copy(rv(c, tt * 128, 128), bank[:, 0:128]), c_ve)

    st_state = {"bout_prev": 0, "srow_sc_prev": 0, "pe6": None, "pe7": None,
                "statsb_war": None, "pb56_war": None}

    def stats_compute(w, cols=None, guard_ve=None):
        t0 = w * W
        P.wait("vector", dch["bout"], st_state["bout_prev"])
        P.wait("vector", c_sc, st_state["srow_sc_prev"])
        P.wait("tensor", c_ve, st_state["pb56_war"])
        P.wait("tensor", c_ve, st_state.get("tail_ve"))
        if guard_ve is not None:
            P.wait("tensor", c_ve, guard_ve)
        P.wait("sync", dch["sbst"], dch["sbst"].n)
        pe_m = {}
        for c in range(8):
            sl = c % 2
            if c >= 2:
                P.wait("vector", c_pe, pe_m[c - 2])
            elif st_state["pe6"] is not None:
                P.wait("vector", c_pe, st_state["pe6"] if c == 0 else st_state["pe7"])
            xsl = xbt[:, sl * 512:(sl + 1) * 512]
            qsl = sqt[:, sl * 512:(sl + 1) * 512]
            P.put("vector", op_copy(xsl, rv(c, t0, W)))
            vvm = P.put("vector", op_tt(qsl, xsl, xsl, ALU.mult), c_ve)
            P.wait("tensor", c_ve, vvm)
            P.put("tensor", op_mm(pb[6][0:1, :], onescol_bf[:], xsl, c == 0, c == 7))
            pe_m[c] = P.put("tensor", op_mm(pb[7][0:1, :], onescol_bf[:], qsl,
                                            c == 0, c == 7), c_pe)
        st_state["pe6"], st_state["pe7"] = pe_m[6], pe_m[7]
        return stats_tail(w, cols)

    def stats_stage(w, arena):
        t0 = w * W
        P.wait("vector", c_pe, state["last_gu_mark"])
        mk = None
        for c in range(8):
            xsl = arena[:, c * 1024: c * 1024 + 512]
            qsl = arena[:, c * 1024 + 512: (c + 1) * 1024]
            P.put("vector", op_copy(xsl, rv(c, t0, W)))
            mk = P.put("vector", op_tt(qsl, xsl, xsl, ALU.mult), c_ve)
        return mk

    def stats_mms(w, arena, stage_mark, cols=None, guard_ve=None):
        P.wait("vector", dch["bout"], st_state["bout_prev"])
        P.wait("vector", c_sc, st_state["srow_sc_prev"])
        P.wait("tensor", c_ve, st_state["pb56_war"])
        P.wait("tensor", c_ve, st_state.get("tail_ve"))
        if guard_ve is not None:
            P.wait("tensor", c_ve, guard_ve)
        P.wait("tensor", c_ve, stage_mark)
        P.wait("sync", dch["sbst"], dch["sbst"].n)
        for c in range(8):
            P.put("tensor", op_mm(pb[6][0:1, :], onescol_bf[:],
                                  arena[:, c * 1024: c * 1024 + 512], c == 0, c == 7))
            pe_m = P.put("tensor", op_mm(pb[7][0:1, :], onescol_bf[:],
                                         arena[:, c * 1024 + 512: (c + 1) * 1024],
                                         c == 0, c == 7), c_pe if c == 7 else None)
        st_state["pe6"] = st_state["pe7"] = pe_m
        return stats_tail(w, cols)

    def stats_tail(w, cols):
        P.wait("vector", c_pe, st_state["pe7"])
        P.put("vector", op_ts(stats_row[0:1, 0:512], pb[6][0:1, :], 1.0 / D, ALU.mult))
        P.put("vector", op_stt(srow[0:1, :], stats_row[0:1, 0:512], 1.0,
                               stats_row[0:1, 0:512], ALU.mult, ALU.mult))
        vvm = P.put("vector", op_stt(srow[0:1, :], pb[7][0:1, :], 1.0 / D, srow[0:1, :],
                                     ALU.mult, ALU.subtract), c_ve)
        st_state["tail_ve"] = vvm
        P.wait("scalar", c_ve, vvm)
        P.put("scalar", op_act(srow[0:1, :], srow[0:1, :], AF.Ln, bias=f32(1e-5)))
        sc = P.put("scalar", op_act(stats_row[0:1, 512:1024], srow[0:1, :], AF.Exp,
                                    scale=f32(-0.5)), c_sc)
        st_state["srow_sc_prev"] = sc
        P.wait("sync", c_sc, sc)
        st_state["bout_prev"] = P.put("sync", op_dma(bnc_st_d[w:w + 1, :], stats_row[:]),
                                      dch["bout"], 16)
        P.wait("sync", dch["bout"], st_state["bout_prev"])
        if cols is not None:
            mcol, rcol, wq = cols
            if state["col_read"] is not None:
                P.wait("gpsimd", c_ve, state["col_read"])
            P.wait("gpsimd", dch["bout"], st_state["bout_prev"])
            src_m = bnc_st_d[w:w + 1, 0:512].rearrange("o (a p) -> (o p) a", p=128)
            src_r = bnc_st_d[w:w + 1, 512:1024].rearrange("o (a p) -> (o p) a", p=128)
            P.put("gpsimd", op_dma(mcol[:, wq * 4:wq * 4 + 4], src_m,
                                   allow_slow_non_contiguous=True), dch["col"], 16)
            P.put("gpsimd", op_dma(rcol[:, wq * 4:wq * 4 + 4], src_r,
                                   allow_slow_non_contiguous=True), dch["col"], 16)
        return st_state["bout_prev"]

    def statsb_load(w):
        P.wait("sync", dch["bout"], stats_done[w])
        P.wait("sync", c_ve, st_state["statsb_war"])
        return P.put("sync", op_dma(statsb[:],
                                    bnc_st_d[w:w + 1, :].to_broadcast((128, 1024))),
                     dch["sbst"], 16)

    stats_done = {}

    gu_use = {}
    dn_use = {}
    cwb_use = {}
    state = {"col_read": None, "cwout_prev": 0, "last_gu_mark": None, "gu_rr": 0, "dn_rr": 0}
    sil_state = {}

    def emit_ffn(fb, tail_prefetch=False, convert=False):
        xn, ht, gub, dnb = fb["xn"], fb["ht"], fb["gu"], fb["dn"]
        cwb, cwT = fb["cwb"], fb["cwT"]
        silu_sb = fb["silu"]
        mcol, rcol, uneg = fb["mcol"], fb["rcol"], fb["uneg"]
        t1 = fb["t1"]
        t8a, lg8, ex8, ge8, srt8, cwbf = (fb["t8a"], fb["lg8"], fb["ex8"],
                                          fb["ge8"], fb["srt8"], fb["cwbf"])

        for wq in range(2):
            stats_done[wq] = stats_compute(wq, cols=(fb["mcol"], fb["rcol"], wq))
        stats_prefetch = [
            [(2, (fb["mcol"], fb["rcol"], 0)), (3, (fb["mcol"], fb["rcol"], 1))],
            [(4, (fb["mcol"], fb["rcol"], 0)), (5, (fb["mcol"], fb["rcol"], 1))],
            [(6, (fb["mcol"], fb["rcol"], 0)), (7, (fb["mcol"], fb["rcol"], 1))],
            [(w, None) for w in range(4)] if tail_prefetch else [],
        ]
        wb_n = {i: 0 for i in range(6)}
        staged_stats = []
        pend_stats = []
        for q in range(4):
            raw_w = convert and q == 0
            if convert and q == 1:
                for i in range(6):
                    P.wait("sync", dch["wb%d" % i], dch["wb%d" % i].n)
                    P.wait("gpsimd", dch["wb%d" % i], dch["wb%d" % i].n)
            tq0 = q * QT
            norm_mark = None
            for wq in range(2):
                w = q * 2 + wq
                vs = statsb_load(w)
                P.wait("vector", dch["sbst"], vs)
                P.wait("vector", c_pe, state["last_gu_mark"])
                P.wait("vector", c_pe, state.get("stats_mm_last"))
                for c in range(8):
                    P.put("vector", op_tt(t1[:], rv(c, tq0 + wq * W, W),
                                          statsb[:, 0:512], ALU.subtract))
                    norm_mark = P.put("vector", op_tt(
                        xn[:, c * QT + wq * W: c * QT + wq * W + W],
                        t1[:], statsb[:, 512:1024], ALU.mult), c_ve)
                st_state["statsb_war"] = norm_mark

            P.wait("vector", dch["col"], dch["col"].n)
            P.wait("vector", dch["cwout"], state["cwout_prev"])
            lgps = pb[4][:, 0:64]
            P.wait("tensor", c_ve, state.get("dn_done"))
            P.wait("tensor", c_sc, sil_state.get(1))
            for tt in range(8):
                for kc in range(8):
                    m = P.put("tensor", op_mm(lgps[:, tt * 8:(tt + 1) * 8],
                                              rv(kc, tq0 + tt * 128, 128),
                                              wr_sb[:, kc * 8:kc * 8 + 8],
                                              kc == 0, kc == 7),
                              c_pe if (kc == 7 and tt == 7) else None)
            P.wait("vector", c_pe, m)

            def bview(t):
                return t[:].unsqueeze(2).broadcast_to((128, 8, 8))

            def eview(t):
                return t[:].unsqueeze(1).broadcast_to((128, 8, 8))
            lg3 = lg8[:].rearrange("p (a b) -> p a b", b=8)
            P.put("vector", op_tt(lg3, lgps.rearrange("p (a b) -> p a b", b=8),
                                  bview(rcol), ALU.mult))
            P.put("vector", op_stt(uneg[:], mcol[:], -1.0, rcol[:], ALU.mult, ALU.mult))
            P.put("vector", op_tt(t8a[:].rearrange("p (a b) -> p a b", b=8), eview(sb8),
                                  bview(uneg), ALU.mult))
            lg_read = P.put("vector", op_tt(lg8[:], lg8[:], t8a[:], ALU.add), c_ve)
            P.put("vector", op_tt(lg3, lg3, eview(rb_b), ALU.add))
            for tt in range(8):
                P.put("vector", op_max8(srt8[:, tt * 8:(tt + 1) * 8],
                                        lg8[:, tt * 8:(tt + 1) * 8]), selfdep=(tt == 0))
            m2view = srt8[:, 1:64:8].unsqueeze(2).broadcast_to((128, 8, 8))
            P.put("vector", op_tt(ge8[:].rearrange("p (a b) -> p a b", b=8), lg3,
                                  m2view, ALU.is_ge))
            vvm = P.put("vector", op_copy(ex8[:], lg8[:]), c_ve)
            P.wait("scalar", c_ve, vvm)
            sc = P.put("scalar", op_act(ex8[:], ex8[:], AF.Exp, scale=f32(-1.0)), c_sc)
            P.wait("vector", c_sc, sc)
            P.put("vector", op_ts(ex8[:], ex8[:], 1.0, ALU.add))
            P.put("vector", op_recip(ex8[:], ex8[:]))
            vvm = P.put("vector", op_stt(cwbf[:], ex8[:], 0.5, ge8[:],
                                         ALU.mult, ALU.mult), c_ve)
            P.wait("tensor", c_ve, vvm)
            P.wait("tensor", c_ve, state.get("tp_read") or 0)
            for tt in range(8):
                m = P.put("tensor", op_tp(tp_big[0:8, tt * 128:(tt + 1) * 128],
                                          cwbf[:, tt * 8:(tt + 1) * 8], ident_bf[:]),
                          c_pe if tt == 7 else None)
            P.wait("vector", c_pe, m)
            tp_read = P.put("vector", op_copy(cwT[:], tp_big), c_ve)
            state["tp_read"] = tp_read
            state["col_read"] = lg_read
            P.wait("sync", c_ve, tp_read)
            vcw = P.put("sync", op_dma(bnc_cw_d[:], cwT[:]), dch["cwout"], 16)
            state["cwout_prev"] = vcw
            if dbg_on and q == 0:
                P.put("sync", op_dma(dbgcw_d, cwT[:]), dch["cwout"], 16)
                P.put("sync", op_dma(dbgcol_d[:, 0:8], mcol[:]), dch["cwout"], 16)
                P.put("sync", op_dma(dbgcol_d[:, 8:16], rcol[:]), dch["cwout"], 16)
                P.put("sync", op_dma(dbgcol_d[:, 16:24], uneg[:]), dch["cwout"], 16)
                state["cwout_prev"] = dch["cwout"].n

            cwb_ready = {}

            def issue_cwb(re):
                slot = re % 2
                ch = dch["cwb%d" % slot]
                P.wait("sync", dch["cwout"], vcw)
                if slot in cwb_use:
                    P.wait("sync", c_ve, cwb_use[slot])
                cwb_ready[re] = P.put("sync", op_dma(
                    cwb[:, slot * QT:(slot + 1) * QT],
                    bnc_cw_d[re:re + 1, :].to_broadcast((128, QT))), ch, 16)

            n_routed = max(0, n_exp - 1)
            if n_routed > 0:
                issue_cwb(0)

            h_last = norm_mark
            h_evs = {0: norm_mark, 1: norm_mark, 2: norm_mark}
            sil_mark = {}
            usb_mark = {}
            for ei in range(n_exp):
                routed = ei > 0
                re = ei - 1
                e = ei
                if routed and re + 1 < n_routed:
                    issue_cwb(re + 1)
                for half in range(2):
                    h_marks = {}
                    for icl in range(8):
                        ic = half * 8 + icl
                        gi = state["gu_rr"] % 3
                        state["gu_rr"] += 1
                        par = ic % 2
                        gps = pbp[par * 2]
                        ups = pbp[par * 2 + 1]
                        slot = gub[:, gi * 2048:(gi + 1) * 2048]
                        if raw_w:
                            ch = dch["gu%d" % gi]
                            P.wait("gpsimd", c_pe, gu_use.get(gi))
                            P.wait("gpsimd", dch["wb%d" % gi], wb_n[gi])
                            vgu = P.put("gpsimd", op_dma(slot, gu_d[e, ic]), ch, 16)
                            P.wait("sync", ch, vgu)
                            wb_n[gi] = P.put("sync", op_dma(gu_bf_d[e, ic], slot),
                                             dch["wb%d" % gi], 16)
                        elif ic % 2 == 0:
                            ch = dch["gus%d" % gi]
                            P.wait("sync", c_pe, gu_use.get(gi))
                            vgu = P.put("sync", op_dma(slot, gu_bf_d[e, ic]), ch, 16)
                        else:
                            ch = dch["gu%d" % gi]
                            P.wait("gpsimd", c_pe, gu_use.get(gi))
                            vgu = P.put("gpsimd", op_dma(slot, gu_bf_d[e, ic]), ch, 16)
                        P.wait("tensor", ch, vgu)
                        if par in sil_mark:
                            P.wait("tensor", c_sc, sil_mark[par])
                            P.wait("tensor", c_ve, usb_mark[par])
                        if icl == 0:
                            P.wait("tensor", c_ve, h_evs[0])
                            P.wait("tensor", c_ve, h_evs[1])
                        elif icl == 1:
                            P.wait("tensor", c_ve, h_evs[2])
                            P.wait("tensor", c_ve, state.get("tp_read"))
                        for kc in range(8):
                            for w2 in range(2):
                                P.put("tensor", op_mm(
                                    gps[:, w2 * W:(w2 + 1) * W],
                                    gub[:, gi * 2048 + kc * 128: gi * 2048 + (kc + 1) * 128],
                                    xn[:, kc * QT + w2 * W: kc * QT + w2 * W + W],
                                    kc == 0, kc == 7))
                        for kc in range(8):
                            for w2 in range(2):
                                m = P.put("tensor", op_mm(
                                    ups[:, w2 * W:(w2 + 1) * W],
                                    gub[:, gi * 2048 + 1024 + kc * 128:
                                        gi * 2048 + 1024 + (kc + 1) * 128],
                                    xn[:, kc * QT + w2 * W: kc * QT + w2 * W + W],
                                    kc == 0, kc == 7),
                                    c_pe if (kc == 7 and w2 == 1) else None)
                        gu_use[gi] = m
                        state["last_gu_mark"] = m
                        P.wait("scalar", c_pe, m)
                        if par in usb_mark:
                            P.wait("scalar", c_ve, usb_mark[par])
                        sil_mark[par] = P.put("scalar", op_act(
                            silu_sb[:, par * 1024:(par + 1) * 1024], gps[:], AF.Silu), c_sc)
                        sil_state[par] = sil_mark[par]
                        P.wait("vector", c_sc, sil_mark[par])
                        if routed and ic == 0:
                            P.wait("vector", dch["cwb%d" % (re % 2)], cwb_ready[re])
                        hsl = ht[:, icl * QT: (icl + 1) * QT]
                        u_mark = P.put("vector", op_stt(
                            hsl, ups[:], 1.0 if routed else 0.5,
                            silu_sb[:, par * 1024:(par + 1) * 1024],
                            ALU.mult, ALU.mult), c_ve)
                        if routed:
                            sl = re % 2
                            u_mark = P.put("vector", op_tt(
                                hsl, hsl, cwb[:, sl * QT:(sl + 1) * QT], ALU.mult), c_ve)
                        usb_mark[par] = u_mark
                        h_marks[icl] = u_mark
                        if routed:
                            cwb_use[re % 2] = u_mark
                    h_ready = u_mark
                    dn_evac = {}
                    dn_rot = [0, 1, 2, 0, 1, 0, 1, 2]
                    for hc in range(8):
                        di = state["dn_rr"] % 3
                        state["dn_rr"] += 1
                        slot = dnb[:, di * 1024:(di + 1) * 1024]
                        if raw_w:
                            ch = dch["dn%d" % di]
                            P.wait("gpsimd", c_pe, dn_use.get(di))
                            P.wait("gpsimd", dch["wb%d" % (3 + di)], wb_n[3 + di])
                            vdn = P.put("gpsimd", op_dma(slot, dn_d[e, half, hc]), ch, 16)
                            P.wait("sync", ch, vdn)
                            wb_n[3 + di] = P.put("sync", op_dma(dn_bf_d[e, half, hc], slot),
                                                 dch["wb%d" % (3 + di)], 16)
                        elif hc % 2 == 0:
                            ch = dch["dns%d" % di]
                            P.wait("sync", c_pe, dn_use.get(di))
                            vdn = P.put("sync", op_dma(slot, dn_bf_d[e, half, hc]), ch, 16)
                        else:
                            ch = dch["dn%d" % di]
                            P.wait("gpsimd", c_pe, dn_use.get(di))
                            vdn = P.put("gpsimd", op_dma(slot, dn_bf_d[e, half, hc]), ch, 16)
                        P.wait("tensor", ch, vdn)
                        dpair = pbp[dn_rot[hc]]
                        if dn_rot[hc] in dn_evac:
                            P.wait("tensor", c_ve, dn_evac[dn_rot[hc]])
                        elif dn_rot[hc] == 0:
                            P.wait("tensor", c_sc, sil_mark.get(0))
                        elif dn_rot[hc] == 1:
                            P.wait("tensor", c_ve, h_marks[6])
                        elif dn_rot[hc] == 2:
                            P.wait("tensor", c_sc, sil_mark.get(1))
                        for icl2 in range(8):
                            if hc == 0:
                                P.wait("tensor", c_ve, h_marks[icl2])
                            for w2 in range(2):
                                m = P.put("tensor", op_mm(
                                    dpair[:, w2 * W:(w2 + 1) * W],
                                    dnb[:, di * 1024 + icl2 * 128: di * 1024 + (icl2 + 1) * 128],
                                    ht[:, icl2 * QT + w2 * W: icl2 * QT + w2 * W + W],
                                    icl2 == 0, icl2 == 7),
                                    c_pe if (icl2 == 7 and w2 == 1) else None)
                        dn_use[di] = m
                        P.wait("vector", c_pe, m)
                        dn_evac[dn_rot[hc]] = P.put("vector", op_tt(
                            rv(hc, tq0, QT), rv(hc, tq0, QT), dpair[:], ALU.add), c_ve)
                        if (ei == n_exp - 1 and half == 1 and hc == 1
                                and stats_prefetch):
                            pend_stats = stats_prefetch.pop(0)
                            if pend_stats:
                                w2f, colspec = pend_stats[0]
                                staged_stats = [(w2f, colspec,
                                                 stats_stage(w2f, xn), xn)]
                    h_last = dn_evac[2]
                    h_evs = dict(dn_evac)
                    state["dn_done"] = h_last
                    if ei == n_exp - 1 and half == 1 and staged_stats:
                        arenas = (xn, ht)
                        for i, (w2f, colspec) in enumerate(pend_stats):
                            if i > 0:
                                ar = arenas[i % 2]
                                P.wait("vector", c_pe, m)
                                P.wait("vector", c_pe, state.get("stats_mm_last"))
                                smk = stats_stage(w2f, ar)
                            else:
                                _, _, smk, ar = staged_stats[0]
                            stats_done[w2f] = stats_mms(
                                w2f, ar, smk, cols=colspec,
                                guard_ve=usb_mark.get(1))
                            state["stats_mm_last"] = st_state["pe7"]
                        staged_stats = []
                        pend_stats = []
            st_state["pb56_war"] = h_last

    ffn_alloc_n = [0]

    def alloc_ffn(st):
        ffn_alloc_n[0] += 1
        sfx = str(ffn_alloc_n[0])
        return dict(
            xn=sb("xn" + sfx, [128, 8 * QT], BF16, st),
            ht=sb("ht" + sfx, [128, 8 * QT], BF16, st),
            gu=sb("gub" + sfx, [128, 3 * 2048], BF16, st),
            dn=sb("dnb" + sfx, [128, 3 * 1024], BF16, st),
            cwb=sb("cwbb" + sfx, [128, 2 * QT], BF16, st),
            cwT=sb("cwT" + sfx, [8, 1024], BF16, st),
            silu=sb("silu_sb" + sfx, [128, 2048], BF16, st),
            mcol=sb("mcol" + sfx, [128, 8], F32, st),
            rcol=sb("rcol" + sfx, [128, 8], F32, st),
            uneg=sb("uneg" + sfx, [128, 8], F32, st),
            t8a=sb("t8a" + sfx, [128, 64], F32, st),
            lg8=sb("lg8" + sfx, [128, 64], F32, st),
            ex8=sb("ex8" + sfx, [128, 64], F32, st),
            ge8=sb("ge8" + sfx, [128, 64], F32, st),
            srt8=sb("srt8" + sfx, [128, 64], F32, st),
            cwbf=sb("cwbf" + sfx, [128, 64], BF16, st),
            t1=sb("t1" + sfx, [128, 512], F32, st),
        )

    io_st.close()

    if do_ffn0:
        ffn_st = ExitStack()
        fb = alloc_ffn(ffn_st)
        engine_guard("gpsimd")
        dma_quiesce("gpsimd")
        emit_ffn(fb, tail_prefetch=do_attn, convert=True)
        ffn_st.close()

    if do_attn:
        ast = ExitStack()
        rx = sb("rx", [128, 8 * S], BF16, ast)

        engine_guard("vector")
        engine_guard("gpsimd")
        engine_guard("sync")
        dma_quiesce("sync")
        dma_quiesce("gpsimd")

        for w_ in range(4, 8):
            stats_done.pop(w_, None)
        qkv_use = None
        wo_use = None
        state_att = {}
        rope_war = None
        oproj_done = None
        prev_att15 = None
        prev_attcopy = None
        prev_recip = None
        for b in range(BPC):
            bt0 = b * S
            stats_st = ExitStack()
            cs = sb(f"cs{b}", [128, 1024], F32, stats_st)
            nf = sb(f"nf{b}", [128, 1024], F32, stats_st)
            tsc = sb(f"tsc{b}", [128, 1024], F32, stats_st)
            P.wait("sync", c_pe, c_pe.n)
            P.wait("sync", c_ve, c_ve.n)
            P.wait("vector", c_pe, c_pe.n)
            P.wait("vector", c_sc, c_sc.n)
            for wl in range(4):
                w = b * 4 + wl
                st_state["pb56_war"] = oproj_done or st_state["pb56_war"]
                if w not in stats_done:
                    stats_done[w] = stats_compute(w)
                vs = statsb_load(w)
                stats_done.pop(w)
                P.wait("vector", dch["sbst"], vs)
                if rope_war is not None:
                    P.wait("vector", c_pe, rope_war)
                    rope_war = None
                for cp in range(4):
                    P.put("sync", op_dma(cs[:, 0:512],
                                         cos_d[cp, :, wl * W:(wl + 1) * W]), dch["cs"], 16)
                    vcs = P.put("sync", op_dma(cs[:, 512:1024],
                                               sin_d[cp, :, wl * W:(wl + 1) * W]), dch["cs"], 16)
                    P.wait("vector", dch["cs"], vcs)
                    na = nf[:, 0:512]
                    nb2 = nf[:, 512:1024]
                    P.put("vector", op_tt(na, rv(cp, bt0 + wl * W, W),
                                          statsb[:, 0:512], ALU.subtract))
                    P.put("vector", op_tt(na, na, statsb[:, 512:1024], ALU.mult))
                    P.put("vector", op_tt(nb2, rv(cp + 4, bt0 + wl * W, W),
                                          statsb[:, 0:512], ALU.subtract))
                    P.put("vector", op_tt(nb2, nb2, statsb[:, 512:1024], ALU.mult))
                    u1 = tsc[:, 0:512]
                    u2 = tsc[:, 512:1024]
                    P.put("vector", op_tt(u1, na, cs[:, 0:512], ALU.mult))
                    P.put("vector", op_tt(u2, nb2, cs[:, 512:1024], ALU.mult))
                    P.put("vector", op_tt(rx[:, cp * S + wl * W: cp * S + wl * W + W],
                                          u1, u2, ALU.subtract))
                    P.put("vector", op_tt(u1, na, cs[:, 512:1024], ALU.mult))
                    P.put("vector", op_tt(u2, nb2, cs[:, 0:512], ALU.mult))
                    sb_war = P.put("vector", op_tt(
                        rx[:, (cp + 4) * S + wl * W: (cp + 4) * S + wl * W + W],
                        u1, u2, ALU.add), c_ve)
                    P.wait("sync", c_ve, sb_war)
                st_state["statsb_war"] = sb_war
            stats_st.close()

            hp_st = ExitStack()
            k2 = sb(f"k2_{b}", [128, S], BF16, hp_st)
            v2 = sb(f"v2_{b}", [128, 16 * 130], BF16, hp_st)
            qb = sb(f"qb{b}", [128, 1024], BF16, hp_st)
            qkvb = sb(f"qkvb{b}", [128, 3072], BF16, hp_st)
            wob = sb(f"wob{b}", [64, 2048], BF16, hp_st)
            psb = sb(f"psb{b}", [128, 4096], BF16, hp_st)
            a0 = psb[0:64, 0:1024]
            a1 = psb[0:64, 2048:3072]
            rcp = sb(f"rcp{b}", [65, 1024], BF16, hp_st)
            P.wait("gpsimd", dch["cs"], dch["cs"].n)
            P.wait("gpsimd", c_ve, c_ve.n)
            P.wait("vector", dch["cs"], dch["cs"].n)
            mset = None
            for kt in range(16):
                P.put("vector", op_memset(v2[:, kt * 130 + 64: kt * 130 + 65], 1.0))
                mset = P.put("vector", op_memset(v2[:, kt * 130 + 129: kt * 130 + 130], 1.0), c_ve)
            v2_ready = mset

            for hp in range(8):
                if qkv_use is not None:
                    P.wait("gpsimd", c_pe, qkv_use)
                vq = P.put("gpsimd", op_dma(qkvb[:], qkv_d[hp]), dch["qkv"], 16)
                if wo_use is not None:
                    P.wait("gpsimd", c_pe, wo_use)
                vw = P.put("gpsimd", op_dma(wob[:], wo_d[hp]), dch["wo"], 16)
                P.wait("tensor", dch["qkv"], vq)
                if prev_attcopy is not None:
                    P.wait("tensor", c_sc, prev_attcopy)
                    P.wait("tensor", c_ve, prev_recip)
                kv_evac = {}
                for wl in range(4):
                    bank = pb[wl % 2]
                    if wl >= 2:
                        P.wait("tensor", c_ve, kv_evac[wl - 2])
                    for kc in range(8):
                        m = P.put("tensor", op_mm(
                            bank[:], qkvb[:, 1024 + kc * 128: 1024 + (kc + 1) * 128],
                            rx[:, kc * S + wl * W: kc * S + wl * W + W],
                            kc == 0, kc == 7), c_pe if kc == 7 else None)
                    P.wait("vector", c_pe, m)
                    kv_evac[wl] = P.put("vector", op_ts(
                        k2[:, wl * W:(wl + 1) * W], bank[:], bk_c[:, hp:hp + 1], ALU.add), c_ve)
                if hp == 0:
                    P.wait("tensor", c_ve, v2_ready)
                vbanks = (pb[2], pb[3], pb[6], pb[7])
                P.wait("tensor", c_ve, oproj_done)
                for vt in range(16):
                    bank = vbanks[vt % 4]
                    if vt >= 4:
                        P.wait("tensor", c_ve, kv_evac[("v", vt - 4)])
                    for kc in range(8):
                        m = P.put("tensor", op_mm(
                            bank[:, 0:128], rx[:, kc * S + vt * 128: kc * S + (vt + 1) * 128],
                            qkvb[:, 2048 + kc * 128: 2048 + (kc + 1) * 128],
                            kc == 0, kc == 7), c_pe if kc == 7 else None)
                    P.wait("vector", c_pe, m)
                    P.put("vector", op_tt(v2[:, vt * 130: vt * 130 + 64], bank[:, 0:64],
                                          bv_b[:, hp * 128: hp * 128 + 64], ALU.add))
                    kv_evac[("v", vt)] = P.put("vector", op_tt(
                        v2[:, vt * 130 + 65: vt * 130 + 129], bank[:, 64:128],
                        bv_b[:, hp * 128 + 64: hp * 128 + 128], ALU.add), c_ve)
                P.wait("tensor", c_ve, kv_evac[("v", 15)])
                P.wait("tensor", c_ve, kv_evac[3])

                exp_prev_h = None
                def q_build(wlp, exp_guard=None):
                    nonlocal qkv_use
                    for w2 in range(2):
                        wl = wlp * 2 + w2
                        bank = pb[w2]
                        if exp_guard is not None:
                            P.wait("tensor", c_sc, exp_guard)
                        elif w2 == 0:
                            P.wait("tensor", c_ve, kv_evac.get(3))
                        for kc in range(8):
                            m = P.put("tensor", op_mm(
                                bank[:], qkvb[:, kc * 128:(kc + 1) * 128],
                                rx[:, kc * S + wl * W: kc * S + wl * W + W],
                                kc == 0, kc == 7), c_pe if kc == 7 else None)
                        P.wait("vector", c_pe, m)
                        kv_evac[("q", w2)] = P.put("vector", op_ts(
                            qb[:, w2 * W:(w2 + 1) * W], bank[:],
                            bq_c[:, hp:hp + 1], ALU.add), c_ve)
                    qkv_use = m

                prev_exp = state_att.get("prev_exp", {})
                for wlp in range(2):
                    if wlp == 0:
                        q_build(0)
                    P.wait("tensor", c_ve, kv_evac[("q", 1)])

                    attcopy_h = {}
                    recip_h = {}
                    attn_m = {}
                    pe_S = {}
                    pe_att = {}
                    exp_m = {}

                    def s_pair(h):
                        return pbp[0] if h == 0 else pbp[2]

                    def att_banks(h):
                        return (pb[2], pb[3]) if h == 0 else (pb[6], pb[7])

                    def emit_S(h, kt):
                        sp = s_pair(h)
                        if kt >= 1:
                            P.wait("tensor", c_sc, exp_m[(h, kt - 1)])
                        else:
                            if (h, 15) in prev_exp:
                                P.wait("tensor", c_sc, prev_exp[(h, 15)])
                            P.wait("tensor", c_ve, oproj_done)
                        P.put("tensor", op_mm(
                            sp[:, 0:512], k2[h * 64:(h + 1) * 64, kt * 128:(kt + 1) * 128],
                            qb[h * 64:(h + 1) * 64, 0:512], True, True))
                        pe_S[(h, kt)] = P.put("tensor", op_mm(
                            sp[:, 512:1024], k2[h * 64:(h + 1) * 64, kt * 128:(kt + 1) * 128],
                            qb[h * 64:(h + 1) * 64, 512:1024], True, True), c_pe)

                    def emit_exp(h, kt):
                        sp = s_pair(h)
                        P.wait("scalar", c_pe, pe_S[(h, kt)])
                        if kt >= 2:
                            P.wait("scalar", c_pe, pe_att[(h, kt - 2)])
                        else:
                            if state_att.get("patt%d" % h) is not None:
                                P.wait("scalar", c_pe, state_att["patt%d" % h])
                            P.wait("scalar", c_pe, wo_use)
                        sl0 = (h * 2 + kt % 2) * 1024
                        exp_m[(h, kt)] = P.put("scalar", op_act(
                            psb[:, sl0: sl0 + 1024], sp[:], AF.Exp,
                            bias=f32(-20.0), scale=f32(0.125)), c_sc)

                    def emit_att(h, kt):
                        ab = att_banks(h)
                        P.wait("tensor", c_sc, exp_m[(h, kt)])
                        if kt == 0:
                            P.wait("tensor", c_sc, state_att.get("copy%d" % h))
                            P.wait("tensor", c_ve, state_att.get("rdTT%d" % h))
                            P.wait("tensor", c_ve, oproj_done)
                        sl0 = (h * 2 + kt % 2) * 1024
                        for w2 in range(2):
                            mm_m = P.put("tensor", op_mm(
                                ab[w2][0:65, :],
                                v2[:, kt * 130 + h * 65: kt * 130 + (h + 1) * 65],
                                psb[:, sl0 + w2 * 512: sl0 + (w2 + 1) * 512],
                                kt == 0, kt == 15),
                                c_pe if w2 == 1 else None)
                        pe_att[(h, kt)] = mm_m

                    for h in range(2):
                        emit_S(h, 0)
                        emit_exp(h, 0)
                    for h in range(2):
                        emit_S(h, 1)
                        emit_exp(h, 1)
                    for kt in range(16):
                        for h in range(2):
                            emit_att(h, kt)
                        if kt + 2 < 16:
                            for h in range(2):
                                emit_S(h, kt + 2)
                                emit_exp(h, kt + 2)

                    def finish_a(h):
                        ah = (a0, a1)[h]
                        ab = att_banks(h)
                        P.wait("scalar", c_pe, pe_att[(h, 15)])
                        P.wait("scalar", c_pe, wo_use)
                        for w2 in range(2):
                            attcopy_h[h] = P.put("scalar", op_scopy(
                                ah[0:64, w2 * 512:(w2 + 1) * 512],
                                ab[w2][0:64, :]), c_sc, selfdep=(w2 == 0))
                        state_att["copy%d" % h] = attcopy_h[h]
                        P.wait("vector", c_pe, pe_att[(h, 15)])
                        P.wait("vector", c_sc, attcopy_h[h])
                        P.wait("vector", c_pe, state_att.get("bc_rd"))
                        for w2 in range(2):
                            recip_h[h] = P.put("vector", op_recip(
                                rcp[64:65, w2 * 512:(w2 + 1) * 512],
                                ab[w2][64:65, :], nc), c_ve, selfdep=(w2 == 0))

                    def finish_b(h):
                        ah = (a0, a1)[h]
                        ab = att_banks(h)
                        P.wait("tensor", c_ve, recip_h[h])
                        bcms = []
                        for w2 in range(2):
                            bcms.append(P.put("tensor", op_mm(
                                ab[w2][0:64, :], ones_bf[64:65, 0:64],
                                rcp[64:65, w2 * 512:(w2 + 1) * 512], True, True), c_pe))
                        state_att["bc_rd"] = bcms[1]
                        for w2 in range(2):
                            P.wait("vector", c_pe, bcms[w2])
                            attn_m[(h, w2)] = P.put("vector", op_tt(
                                ah[0:64, w2 * 512:(w2 + 1) * 512],
                                ah[0:64, w2 * 512:(w2 + 1) * 512],
                                ab[w2][0:64, :], ALU.mult), c_ve)
                        state_att["rdTT%d" % h] = attn_m[(h, 1)]

                    finish_a(0)
                    finish_a(1)
                    if wlp == 0:
                        q_build(1, exp_guard=exp_m[(0, 14)])
                    finish_b(0)
                    finish_b(1)
                    state_att["patt0"] = pe_att[(0, 14)]
                    state_att["patt1"] = pe_att[(1, 14)]
                    prev_exp = {(0, 15): exp_m[(0, 15)], (1, 15): exp_m[(1, 15)]}
                    state_att["prev_exp"] = prev_exp
                    prev_att15 = pe_att[(1, 15)]
                    prev_attcopy = attcopy_h[1]
                    prev_recip = max(recip_h[0], recip_h[1])

                    bank_ids = [2, 3, 4, 5]
                    oproj_evac = {}
                    first_op = True
                    for w2 in range(2):
                        wl = wlp * 2 + w2
                        for fc in range(8):
                            bid = bank_ids[(w2 * 8 + fc) % 4]
                            bank = pb[bid]
                            if first_op:
                                P.wait("tensor", c_sc, exp_m[(1, 15)])
                                P.wait("tensor", c_ve, attn_m[(1, 1)])
                                first_op = False
                            if bid in oproj_evac:
                                P.wait("tensor", c_ve, oproj_evac[bid])
                            P.put("tensor", op_mm(
                                bank[:], wob[0:64, fc * 128:(fc + 1) * 128],
                                a0[0:64, w2 * 512:(w2 + 1) * 512], True, False))
                            m = P.put("tensor", op_mm(
                                bank[:], wob[0:64, 1024 + fc * 128: 1024 + (fc + 1) * 128],
                                a1[0:64, w2 * 512:(w2 + 1) * 512], False, True), c_pe)
                            P.wait("vector", c_pe, m)
                            oproj_evac[bid] = P.put("vector", op_tt(
                                rv(fc, bt0 + wl * W, W), rv(fc, bt0 + wl * W, W),
                                bank[:], ALU.add), c_ve)
                    wo_use = m
                    oproj_done = oproj_evac[bank_ids[(1 * 8 + 7) % 4]]
            hp_st.close()
            for fc in range(8):
                bom = P.put("vector", op_ts(rv(fc, bt0, S), rv(fc, bt0, S),
                                            bo_c[:, fc:fc + 1], ALU.add), c_ve)
            oproj_done = bom
            rope_war = qkv_use
        st_state["pb56_war"] = oproj_done
        ast.close()

    if do_ffn1:
        ffn_st = ExitStack()
        fb = alloc_ffn(ffn_st)
        engine_guard("gpsimd")
        engine_guard("vector")
        dma_quiesce("gpsimd")
        dma_quiesce("sync")
        gu_use.clear()
        dn_use.clear()
        cwb_use.clear()
        gu_use[0] = c_pe.n
        gu_use[1] = c_pe.n
        dn_use[0] = c_pe.n
        dn_use[1] = c_pe.n
        cwb_use[0] = c_ve.n
        cwb_use[1] = c_ve.n
        emit_ffn(fb)
        ffn_st.close()

    io_st2 = ExitStack()
    yio = sb("yio", [128, 2048], F32, io_st2)
    engine_guard("tensor")
    st_marks = {}
    cp_marks = {}
    for tt in range(32):
        sl = tt % 2
        if tt >= 2:
            P.wait("vector", dch["yout%d" % sl], st_marks[tt - 2])
        vm = None
        for c in range(8):
            gidx = tt * 8 + c
            bank = pb[c % 4]
            if gidx - 4 in cp_marks:
                P.wait("tensor", c_ve, cp_marks[gidx - 4])
            m = P.put("tensor", op_tp(bank[:, 0:128], rv(c, tt * 128, 128), ident32[:]), c_pe)
            P.wait("vector", c_pe, m)
            vm = P.put("vector", op_copy(yio[:, sl * 1024 + c * 128: sl * 1024 + (c + 1) * 128],
                                         bank[:, 0:128]), c_ve)
            cp_marks[gidx] = vm
        P.wait("sync", c_ve, vm)
        st_marks[tt] = P.put("sync", op_dma(y_d[tt * 128:(tt + 1) * 128, :],
                                            yio[:, sl * 1024:(sl + 1) * 1024]),
                             dch["yout%d" % sl], 16)
    dma_quiesce("sync")
    io_st2.close()

    with nc.Block() as blk:
        @blk.sync
        def _(e):
            for fop in P.ops["sync"]:
                fop(e)

        @blk.gpsimd
        def _(e):
            for fop in P.ops["gpsimd"]:
                fop(e)

        @blk.tensor
        def _(e):
            for fop in P.ops["tensor"]:
                fop(e)

        @blk.vector
        def _(e):
            for fop in P.ops["vector"]:
                fop(e)

        @blk.scalar
        def _(e):
            for fop in P.ops["scalar"]:
                fop(e)

    es.close()
    return nc



def _host_prep(inputs):
    gu_all = np.stack(
        [_prep_gu(np.asarray(inputs["L0_ffn_Wg_s"]), np.asarray(inputs["L0_ffn_Wu_s"]))] +
        [_prep_gu(np.asarray(inputs["L0_ffn_Wg"])[e], np.asarray(inputs["L0_ffn_Wu"])[e])
         for e in range(NE)])
    dn_all = np.stack(
        [_prep_dn(np.asarray(inputs["L0_ffn_Wd_s"]))] +
        [_prep_dn(np.asarray(inputs["L0_ffn_Wd"])[e]) for e in range(NE)])
    cos_h, sin_h = _prep_rope()
    return dict(
        gu=gu_all, dn=dn_all,
        qkv=_prep_qkv(np.asarray(inputs["L0_attn_Wq"]), np.asarray(inputs["L0_attn_Wk"]),
                      np.asarray(inputs["L0_attn_Wv"])),
        wo=_prep_wo(np.asarray(inputs["L0_attn_Wo"])),
        wr=np.ascontiguousarray(np.asarray(inputs["L0_ffn_Wr"]).T
                                .reshape(8, 128, 8).transpose(1, 0, 2).reshape(128, 64)),
        rbb=np.ascontiguousarray(np.broadcast_to(
            np.asarray(inputs["L0_ffn_rb"], dtype=f32), (128, 8))),
        bq=_prep_bias_cols(np.asarray(inputs["L0_attn_bq"])),
        bk=_prep_bias_cols(np.asarray(inputs["L0_attn_bk"])),
        bo=_prep_bias_cols(np.asarray(inputs["L0_attn_bo"])),
        bvb=np.ascontiguousarray(np.broadcast_to(
            np.asarray(inputs["L0_attn_bv"], dtype=f32), (128, D))),
        cosd=cos_h, sind=sin_h,
        ident=np.eye(128, dtype=f32),
        onesd=np.ones((128, 128), dtype=f32),
    )


def kernel(**inputs):
    xs = np.asarray(inputs["L0_attn_x"], dtype=np.float32)
    shared = _host_prep(inputs)
    nc = build_program()
    in_maps = []
    for c in range(NCORE):
        m = dict(shared)
        m["x"] = np.ascontiguousarray(xs[2 * c:2 * c + 2].reshape(T, D))
        in_maps.append(m)
    res = bass_utils.run_bass_kernel_spmd(nc, in_maps, core_ids=list(range(NCORE)))
    out = np.concatenate([np.asarray(res.results[c]["y"]).reshape(BPC, S, D)
                          for c in range(NCORE)], axis=0)
    return out
